# version 08
# baseline (speedup 1.0000x reference)
import numpy as np
from contextlib import ExitStack
import ml_dtypes
import concourse.bass as bass
import concourse.mybir as mybir
from concourse.bass_core_rust import IndirectOffsetOnAxis
from concourse.bass_utils import run_bass_kernel_spmd


F32 = mybir.dt.float32
BF16 = mybir.dt.bfloat16
U8 = mybir.dt.uint8
AF = mybir.ActivationFunctionType
ALU = mybir.AluOpType
AX = mybir.AxisListType

ENGINES = ("sync", "tensor", "vector", "scalar", "gpsimd")

class Sem:
    __slots__ = ("h", "n", "name")
    def __init__(self, h, name):
        self.h = h; self.n = 0; self.name = name
    def tag(self):
        return (self, self.n)

def _dedup(waits):
    best = {}
    for s, v in waits:
        if s is not None and (s not in best or v > best[s]):
            best[s] = v
    return [(s, v) for s, v in best.items()]

class Prog:
    def __init__(self, nc, stack):
        self.nc = nc
        self.stack = stack
        self.ops = {e: [] for e in ENGINES}
        self.hwm = {e: {} for e in ENGINES}
        self._semcnt = 0
        self.esem = {e: self.sem(f"es_{e}") for e in ENGINES}
        self.dsem = {e: self.sem(f"ds_{e}") for e in ("sync", "gpsimd", "scalar", "vector")}

    def sem(self, name=None):
        if name is None:
            name = f"s{self._semcnt}"
        self._semcnt += 1
        h = self.stack.enter_context(self.nc.semaphore(name))
        return Sem(h, name)

    def filt(self, eng, wl):
        hw = self.hwm[eng]
        out = []
        for s2, v in _dedup(wl):
            if v > hw.get(s2, 0):
                hw[s2] = v
                out.append((s2, v))
        return out

    def emit(self, eng, fn, waits=(), dma=False, inc=1, noinc=False):
        s = self.dsem[eng] if dma else self.esem[eng]
        k = 16 if dma else inc
        wl = list(waits)
        if dma:
            wl.append((s, s.n))
        if not noinc:
            s.n += k
        wl = self.filt(eng, wl)
        def op(e, fn=fn, wl=wl, s=s, k=k, noinc=noinc):
            for ws, wv in wl:
                e.wait_ge(ws.h, wv)
            inst = fn(e)
            if not noinc:
                inst.then_inc(s.h, k)
        self.ops[eng].append(op)
        return (s, s.n)

    def dma(self, out, in_, waits=(), eng="sync"):
        return self.emit(eng, lambda e: e.dma_start(out=out, in_=in_), waits, dma=True)

    def mm(self, out, lhsT, rhs, start, stop, waits=()):
        return self.emit("tensor", lambda e: e.matmul(out, lhsT, rhs, start=start, stop=stop), waits)

    def mmchain(self, out, pairs, waits=()):
        n = len(pairs)
        t = None
        for i, (l, r) in enumerate(pairs):
            t = self.emit("tensor", (lambda e, l=l, r=r, i=i, n=n, out=out:
                          e.matmul(out, l, r, start=(i == 0), stop=(i == n - 1))),
                          waits=(waits if i == 0 else ()), noinc=(i != n - 1))
        return t

    def act(self, out, in_, func, bias=0.0, scale=1.0, waits=(), accum_out=None):
        return self.emit("scalar", lambda e: e.activation(
            out=out, in_=in_, func=func, bias=bias, scale=scale,
            accum_out=accum_out), waits)

    def tt(self, out, in0, in1, op, waits=(), eng="vector"):
        return self.emit(eng, lambda e: e.tensor_tensor(out=out, in0=in0, in1=in1, op=op), waits)

    def ts(self, out, in0, s1, op0, s2=None, op1=None, waits=(), accum_out=None, eng="vector"):
        if s2 is None and accum_out is None:
            return self.emit(eng, lambda e: e.tensor_single_scalar(out=out, in_=in0, scalar=s1, op=op0), waits)
        kw = dict(out=out, in0=in0, scalar1=s1, scalar2=s2, op0=op0, accum_out=accum_out)
        if op1 is not None:
            kw["op1"] = op1
        return self.emit(eng, lambda e: e.tensor_scalar(**kw), waits)

    def stt(self, out, in0, scalar, in1, op0, op1, waits=(), eng="vector"):
        return self.emit(eng, lambda e: e.scalar_tensor_tensor(
            out=out, in0=in0, scalar=scalar, in1=in1, op0=op0, op1=op1), waits)

    def copy(self, out, in_, waits=(), eng="vector"):
        return self.emit(eng, lambda e: e.tensor_copy(out=out, in_=in_), waits)

    def memset(self, ap, val, waits=(), eng="vector"):
        return self.emit(eng, lambda e: e.memset(ap, val), waits)

    def recip(self, out, in_, waits=()):
        return self.emit("vector", lambda e: e.reciprocal(out=out, in_=in_), waits)

    def red(self, out, in_, op, waits=(), negate=None):
        return self.emit("vector", lambda e: e.tensor_reduce(
            out=out, in_=in_, axis=AX.X, op=op, negate=negate), waits)

    def sel(self, out, mask, on_true, on_false, waits=()):
        return self.emit("vector", lambda e: e.select(
            out=out, mask=mask, on_true=on_true, on_false=on_false), waits)

    def transpose(self, out, in_, ident, waits=()):
        return self.emit("tensor", lambda e: e.transpose(out, in_, ident), waits)

    def wait(self, eng, waits):
        wl = self.filt(eng, waits)
        def op(e, wl=wl):
            for ws, wv in wl:
                e.wait_ge(ws.h, wv)
        self.ops[eng].append(op)

    def all_tags(self):
        return [(s, s.n) for s in list(self.esem.values()) + list(self.dsem.values())]

    def build(self):
        nc = self.nc
        with nc.Block() as block:
            @block.sync
            def _(e):
                for op in self.ops["sync"]: op(e)
            @block.tensor
            def _(e):
                for op in self.ops["tensor"]: op(e)
            @block.vector
            def _(e):
                for op in self.ops["vector"]: op(e)
            @block.scalar
            def _(e):
                for op in self.ops["scalar"]: op(e)
            @block.gpsimd
            def _(e):
                for op in self.ops["gpsimd"]: op(e)

def f32(x):
    return np.ascontiguousarray(np.asarray(x), dtype=np.float32)

def bf16(x):
    return np.ascontiguousarray(np.asarray(x).astype(ml_dtypes.bfloat16))

def r3(ap, c, w):
    return ap.rearrange("p (c w) -> p c w", c=c, w=w)

def bc3(ap, c):
    return ap.unsqueeze(1).broadcast_to([ap.shape[0], c, ap.shape[-1]])

def _add_barrier(P):
    tags = P.filt("gpsimd", P.all_tags())
    b = P.sem()
    def gather(e, tags=tags, b=b):
        for ws, wv in tags:
            e.wait_ge(ws.h, wv)
        e.sem_inc(b.h, 1)
    P.ops["gpsimd"].append(gather)
    b.n += 1
    t = (b, b.n)
    for eng in ENGINES:
        if eng != "gpsimd":
            P.wait(eng, [t])
    return t

Prog.barrier = _add_barrier

def _raw(P, eng, fn):
    P.ops[eng].append(fn)

Prog.raw = _raw

class DmaPool:
    def __init__(self, P, name, n):
        self.sems = [P.sem(f"{name}{i}") for i in range(n)]
        self.i = 0
        self.P = P

    def dma(self, out, in_, waits=(), eng="sync"):
        s = self.sems[self.i % len(self.sems)]
        self.i += 1
        wl = list(waits) + [(s, s.n)]
        s.n += 16
        wl = self.P.filt(eng, wl)
        def op(e, wl=wl, s=s, out=out, in_=in_):
            for ws, wv in wl:
                e.wait_ge(ws.h, wv)
            e.dma_start(out=out, in_=in_).then_inc(s.h, 16)
        self.P.ops[eng].append(op)
        return (s, s.n)

def _mkpool(P, name, n):
    return DmaPool(P, name, n)

Prog.pool = _mkpool

def _idma(P, pool, fn, waits=(), eng="gpsimd"):
    s = pool.sems[pool.i % len(pool.sems)]
    pool.i += 1
    wl = list(waits) + [(s, s.n)]
    s.n += 16
    wl = P.filt(eng, wl)
    def op(e, fn=fn, wl=wl, s=s):
        for ws, wv in wl:
            e.wait_ge(ws.h, wv)
        fn(e).then_inc(s.h, 16)
    P.ops[eng].append(op)
    return (s, s.n)

Prog.idma = _idma


NCIN = 8
D = 1024
NH = 8
DH = 128
NE = 8
NHC = 32
S_SEQ = 8192
WID = 768
XW = 513
INV32 = 1.0 / 32.0
INV128 = 1.0 / 128.0
PEXT = (16, 32, 48, 64)
CAP = 2049.0
NITER = 48
NEG = -1e30
AB = 53248


def slab_of(c, i):
    return c % 2, 4 * i + c // 2


def host_prep_fused(ins):
    x = f32(ins["L0_attn_sequence_output"])
    b3 = np.zeros((128, 24), np.float32)
    for ib, b in enumerate((ins["L0_attn_bq"], ins["L0_attn_bk"], ins["L0_attn_bv"])):
        b3[:, 8 * ib:8 * ib + 8] = f32(b).reshape(8, 128).T
    mu3 = np.zeros((128, 16), np.float32)
    mu3[:, 0:8] = f32(ins["L0_ffn_mu_r"]).reshape(8, 128).T
    mu3[:, 8:16] = f32(ins["L0_ffn_mu_k"]).reshape(8, 128).T
    b1p = np.zeros((128, NE * NHC), np.float32)
    for e in range(NE):
        b1p[:, NHC * e:NHC * e + NHC] = f32(ins["L1_ffn_b1"][e]).reshape(NHC, 128).T
    foldT = np.zeros((8, 128), np.float32)
    foldc = np.zeros((128, 8), np.float32)
    for p in range(128):
        foldT[p % 8, p] = 1.0
        foldc[p, p % 8] = 1.0
    cm = np.zeros((4, 128, 512), np.float32)
    for jt in range(4):
        cm[jt] = (128 * jt + np.arange(128)[:, None]) <= np.arange(512)[None, :]
    dims = np.arange(32).astype(np.float32)
    af = np.power(np.float32(10000.0), -dims / np.float32(32.0)).astype(np.float32)
    shared = dict(
        wq=f32(ins["L0_attn_Wq"]), wk=f32(ins["L0_attn_Wk"]), wv=f32(ins["L0_attn_Wv"]),
        b3=b3, bvrow=f32(ins["L0_attn_bv"]).reshape(1, D),
        wr=f32(ins["L0_ffn_W_r"]).T.copy(), wk2=f32(ins["L0_ffn_W_k"]).T.copy(),
        wv2=f32(ins["L0_ffn_W_v"]).T.copy(), mu3=mu3,
        wq1=f32(ins["L1_attn_wq"]), wk1=f32(ins["L1_attn_wk"]),
        wv1=f32(ins["L1_attn_wv"]), wo=f32(ins["L1_attn_wo"]),
        rwT=f32(ins["L1_ffn_router_w"]).T.copy(),
        w1=f32(ins["L1_ffn_w1"]), w2=f32(ins["L1_ffn_w2"]), b1p=b1p,
        b2e=f32(ins["L1_ffn_b2"]),
        ones_bf=bf16(np.ones((128, 1))), ones_f1=f32(np.ones((1, 128))),
        id8=f32(np.eye(8)), idI=f32(np.eye(128)), foldT=foldT, foldc=foldc, cm=bf16(cm),
        trilb=bf16(np.tril(np.ones((128, 128), np.float32)).T),
        ts16=f32(np.triu(np.ones((16, 16), np.float32), 1)),
        tio=np.ascontiguousarray((np.arange(16)[None, :] * 128
                                  + np.arange(128)[:, None]).astype(np.int32)),
        c2048=np.full((1, 768), 2048, np.int32),
        idIb=bf16(np.eye(128)),
    )
    in_maps = []
    for c in range(8):
        xa = np.zeros((4, D, WID), np.float32)
        masks = np.zeros((4, 5, 128, 384), np.float32)
        cs = np.zeros((4, 32, 512), np.float32)
        sn = np.zeros((4, 32, 512), np.float32)
        pmulv = np.zeros((4, 128, 1), np.float32)
        tmask = np.zeros((4, 128, 64), np.float32)
        for i in range(4):
            seq, m = slab_of(c, i)
            start = 512 * m
            lo, hi = start - 128, start + 640
            slo, shi = max(lo, 0), min(hi, S_SEQ)
            xa[i][:, slo - lo: shi - lo] = x[seq, slo:shi, :].T
            for j in range(-1, 4):
                qa = start + 128 * j
                qg = qa + np.arange(128)[None, :]
                qv = (qg >= 0) & (qg < S_SEQ)
                for p in range(3):
                    kg = qa + 128 * (p - 1) + np.arange(128)[:, None]
                    kv = (kg >= 0) & (kg < S_SEQ)
                    mm_ = (np.abs(kg - qg) <= 2) & kv & qv
                    mm_ = np.where(qv, mm_, kv)
                    if j == -1 and p == 0:
                        mm_ = np.zeros_like(mm_)
                    masks[i, j + 1, :, 128 * p:128 * p + 128] = mm_
            pos = (start + np.arange(512)).astype(np.float32)
            rad = (pos[None, :] * af[:, None]).astype(np.float32)
            cs[i] = np.cos(rad.astype(np.float64)).astype(np.float32)
            sn[i] = np.sin(rad.astype(np.float64)).astype(np.float32)
            pmulv[i, :, 0] = 0.0 if m == 0 else 1.0
            E = 4 * m + 4
            tmask[i, :, :] = np.where(np.arange(64)[None, :] < E - 4, 0.0, NEG)
        im = dict(shared)
        im.update(xa=f32(xa), masks=bf16(masks), cs=cs, sn=sn, pmulv=pmulv,
                  tmask=tmask)
        in_maps.append(im)
    return in_maps


def _ln_fm(P, T, xsrc, wid_in, win, dst, pvx):
    obs, o1s, pz, pz2, pp, srow = T["obs"], T["o1s"], T["pz"], T["pz2"], T["pp"], T["srow"]
    rb2, sq2 = T["lnrb"], T["lnsq"]
    for (s0, sw, oc) in win:
        xv = r3(xsrc, NCIN, wid_in)[:, :, s0:s0 + sw]
        rbv = r3(rb2, 8, 512)[:, :, 0:sw]
        sqv = r3(sq2, 8, 512)[:, :, 0:sw]
        tc = P.copy(rbv, xv, waits=[t for t in (pvx.get("m1"),) if t])
        tsq = P.act(sqv, rbv, AF.Square, waits=[tc] + [t for t in (pvx.get("m2"),) if t])
        m1 = m2 = None
        for ci in range(NCIN):
            w0 = [tc] + [t for t in (pvx.get("tm_"),) if t] if ci == 0 else []
            m1 = P.mm(pz[0:1, 0:sw], obs[:], rb2[:, 512 * ci:512 * ci + sw],
                      start=(ci == 0), stop=(ci == NCIN - 1), waits=w0)
        for ci in range(NCIN):
            w0 = [tsq] + [t for t in (pvx.get("t8"),) if t] if ci == 0 else []
            m2 = P.mm(pz2[0:1, 0:sw], obs[:], sq2[:, 512 * ci:512 * ci + sw],
                      start=(ci == 0), stop=(ci == NCIN - 1), waits=w0)
        mslot_ = srow[0:1, 0:sw]
        msq = srow[0:1, 512:512 + sw]
        vars_ = srow[0:1, 1024:1024 + sw]
        rs = srow[0:1, 1536:1536 + sw]
        tm_ = P.ts(mslot_, pz[0:1, 0:sw], 1.0 / D, ALU.mult,
                   waits=[m1] + [t for t in (pvx.get("b1"), pvx.get("t7")) if t])
        t7 = P.tt(msq, mslot_, mslot_, ALU.mult,
                  waits=[tm_] + [t for t in (pvx.get("t8"),) if t])
        t8 = P.stt(vars_, pz2[0:1, 0:sw], 1.0 / D, msq, ALU.mult, ALU.subtract,
                   waits=[t7, m2] + [t for t in (pvx.get("t10"),) if t])
        t8b = P.ts(vars_, vars_, 1e-5, ALU.add, waits=[t8])
        t9 = P.act(vars_, vars_, AF.Sqrt, waits=[t8b])
        t10 = P.recip(rs, vars_, waits=[t9] + [t for t in (pvx.get("b2"),) if t])
        b1_ = P.mm(pp[0][:, 0:sw], o1s[:], mslot_, start=True, stop=True,
                   waits=[tm_] + [t for t in (pvx.get("t11"),) if t])
        b2_ = P.mm(pp[1][:, 0:sw], o1s[:], rs, start=True, stop=True,
                   waits=[t10] + [t for t in (pvx.get("t12"),) if t])
        tv_ = r3(T["lnout"], 8, 512)[:, :, 0:sw]
        t11 = P.tt(tv_, xv, bc3(pp[0][:, 0:sw], 8), ALU.subtract,
                   waits=[b1_] + [t for t in (pvx.get("dma"),) if t])
        t12 = P.tt(tv_, tv_, bc3(pp[1][:, 0:sw], 8), ALU.mult, waits=[t11, b2_])
        tdma = dst(oc, sw, t12)
        pvx.clear()
        pvx.update(m1=m1, m2=m2, tm_=tm_, t7=t7, t8=t8, t10=t10, b1=b1_, b2=b2_,
                   t11=t11, t12=t12, dma=tdma)
    return pvx


def build_fused(dbg=False):
    nc = bass.Bass("TRN2", target_bir_lowering=False)
    st = ExitStack()
    P = Prog(nc, st)
    din = lambda n, s, d: nc.dram_tensor(n, s, d, kind="ExternalInput").ap()

    xa = din("xa", [4, D, WID], F32)
    wq_d = din("wq", [D, D], F32)
    wk_d = din("wk", [D, D], F32)
    wv_d = din("wv", [D, D], F32)
    b3_d = din("b3", [128, 24], F32)
    bvr_d = din("bvrow", [1, D], F32)
    mk_d = din("masks", [4, 5, 128, 384], BF16)
    ob_d = din("ones_bf", [128, 1], BF16)
    o1_d = din("ones_f1", [1, 128], F32)
    wr_d = din("wr", [D, D], F32)
    wk2_d = din("wk2", [D, D], F32)
    wv2_d = din("wv2", [D, D], F32)
    mu_d = din("mu3", [128, 16], F32)
    pm_d = din("pmulv", [4, 128, 1], F32)
    wq1_d = din("wq1", [NH, D, DH], F32)
    wk1_d = din("wk1", [NH, D, DH], F32)
    wv1_d = din("wv1", [NH, D, DH], F32)
    cs_d = din("cs", [4, 32, 512], F32)
    sn_d = din("sn", [4, 32, 512], F32)
    wo_d = din("wo", [D, D], F32)
    cm_d = din("cm", [4, 128, 512], BF16)
    tm_d = din("tmask", [4, 128, 64], F32)
    rw_d = din("rwT", [D, 8], F32)
    id8_d = din("id8", [8, 8], F32)
    idI_d = din("idI", [128, 128], F32)
    fT_d = din("foldT", [8, 128], F32)
    fc_d = din("foldc", [128, 8], F32)
    w1_d = din("w1", [NE, D, 4096], F32)
    w2_d = din("w2", [NE, 4096, D], F32)
    b1_d = din("b1p", [128, NE * NHC], F32)
    b2_d = din("b2e", [8, D], F32)
    trl_d = din("trilb", [128, 128], BF16)
    t16_d = din("ts16", [16, 16], F32)
    tio_d = din("tio", [128, 16], mybir.dt.int32)
    c2k_d = din("c2048", [1, 768], mybir.dt.int32)
    idb_d = din("idIb", [128, 128], BF16)
    out_d = nc.dram_tensor("out", [4, 512, D], F32, kind="ExternalOutput").ap()

    ikind = "ExternalOutput" if dbg else "Internal"
    x1_d = nc.dram_tensor("x1d", [4, D, XW], F32, kind=ikind).ap()
    x2_d = nc.dram_tensor("x2d", [4, D, 512], F32, kind=ikind).ap()
    x3_d = nc.dram_tensor("x3d", [4, D, 512], F32, kind=ikind).ap()
    x3r_d = nc.dram_tensor("x3rd", [2049, D], F32)
    x3rb_d = nc.dram_tensor("x3rbd", [2049, D], BF16)
    grow_d = nc.dram_tensor("growd", [8, 2049], F32)
    ix_d = nc.dram_tensor("ixd", [8, 2049, 1], mybir.dt.int32)
    x4a_d = [nc.dram_tensor("x4ad0", [2049, 512], F32),
             nc.dram_tensor("x4ad1", [2049, 512], F32)]
    qo_d = nc.dram_tensor("qod", [4, NH, DH, 512], BF16).ap()
    kvin = nc.dram_tensor("kvin", [2, NH, 4, 128, 512], BF16)
    kvg = nc.dram_tensor("kvg", [4, 2, NH, 4, 128, 512], BF16)
    nrin = nc.dram_tensor("nrin", [1, 32], F32)
    nrg = nc.dram_tensor("nrg", [4, 32], F32)
    scin = nc.dram_tensor("scin", [4, 8, 512], F32)
    scg = nc.dram_tensor("scg", [4, 4, 8, 512], F32)

    pool_w = P.pool("pw", 8)
    pool_x = P.pool("px", 4)
    pool_kv = P.pool("pkv", 8)
    pool_ix = P.pool("pix", 4)
    pool_gx = P.pool("pgx", 4)
    pool_sc = P.pool("psc", 1)
    sbuf = lambda n, s, d: st.enter_context(nc.sbuf_tensor(n, s, d))
    psum = lambda n, s, d: st.enter_context(nc.psum_tensor(n, s, d))
    b3s = sbuf("b3s", [128, 24], F32)
    bvr = sbuf("bvr", [128, D], F32)
    obs = sbuf("obs", [128, 1], BF16)
    o1s = sbuf("o1s", [1, 128], F32)
    mus = sbuf("mus", [128, 16], F32)
    id8s = sbuf("id8s", [8, 8], F32)
    fTs = sbuf("fTs", [8, 128], F32)
    fcs = sbuf("fcs", [128, 8], F32)
    cms = sbuf("cms", [128, 4 * 512], BF16)
    mrow = sbuf("mrow", [1, 64], F32)
    srow = sbuf("srow", [1, 4 * 512], F32)
    biasM = sbuf("biasM", [128, 1], F32)
    scoTs = sbuf("scoTs", [8, 4 * 512], F32)
    gmT = sbuf("gmT", [8, 4 * 512], F32)
    g_tm = sbuf("g_tm", [128, 128], F32)
    lohi = sbuf("lohi", [8, 4], F32)
    cmpf = sbuf("cmpf", [8, 2], F32)
    nr4 = sbuf("nr4", [1, 128], F32)
    maxk2 = sbuf("maxk2", [1, 8], F32)
    idIsp = sbuf("idIsp", [128, 128], F32)

    B0 = psum("B0", [128, 512], F32)
    B1 = psum("B1", [128, 512], F32)
    B2 = psum("B2", [128, 512], F32)
    B3 = psum("B3", [128, 512], F32)
    B4 = psum("B4", [128, 512], F32)
    B5 = psum("B5", [128, 512], F32)
    R0 = psum("R0", [1, 512], F32)
    PT = psum("PT", [128, 256], BF16)
    R1 = None
    pp = [B0, B1]
    pz = R0
    pz2 = B5[0:1, 0:512]
    pzb = B2

    def at_(name, off, shape, dt):
        h = nc.alloc_sbuf_tensor_at(name, shape, dt, offset=AB + off)
        return h[:]

    P.dma(obs[:], ob_d)
    P.dma(o1s[:], o1_d)
    P.dma(b3s[:], b3_d)
    P.dma(mus[:], mu_d)
    P.dma(id8s[:], id8_d)
    P.dma(fTs[:], fT_d)
    P.dma(fcs[:], fc_d)
    for j in range(4):
        P.dma(cms[:, 512 * j:512 * j + 512], cm_d[j])

    T = dict(obs=obs, o1s=o1s, pz=pz, pz2=pz2, pp=pp, srow=srow)

    xt = at_("xtA", 0, [128, NCIN * WID], F32)
    xtb = at_("xtbA", 24576, [128, NCIN * WID], BF16)
    wst = at_("wstA", 36864, [128, 1024], F32)
    wA = at_("wA", 40960, [128, 3 * NCIN * 1024], BF16)
    qt = at_("qtA", 90112, [128, NCIN * WID], BF16)
    kt = at_("ktA", 102400, [128, NCIN * WID], BF16)
    vt = at_("vtA", 114688, [128, 6 * 1024], BF16)
    mka = at_("mkaA", 126976, [128, 2 * 384], BF16)
    expm = at_("expmA", 128512, [128, 384], BF16)
    zrA = at_("zrA", 129280, [1, 128], F32)
    o_sb = at_("osbA", 129792, [128, 1024], F32)
    t_sbA = at_("tsbA", 133888, [128, 4096], F32)
    sqm = at_("sqm", 166688, [128, 1024], BF16)
    wst2 = at_("wstA2", 168736, [128, 1024], F32)
    lnA1 = at_("lnA1", 90112, [128, 4096], BF16)
    lnA2 = at_("lnA2", 98304, [128, 4096], BF16)
    x1t = at_("x1tA", 150272, [128, NCIN * XW], F32)
    T["lnrb"] = lnA1
    T["lnsq"] = lnA2
    T["lnout"] = t_sbA

    wAq = lambda wi, ci: wA[:, 1024 * (NCIN * wi + ci):1024 * (NCIN * wi + ci) + 1024]
    stgA_last = [None, None]
    cntA = 0
    for wi, wd in enumerate((wq_d, wk_d, wv_d)):
        for ci in range(NCIN):
            par = cntA % 2
            cntA += 1
            stg = (wst, wst2)[par]
            tl = pool_w.dma(stg, wd[128 * ci:128 * ci + 128, :],
                            waits=([stgA_last[par]] if stgA_last[par] else []))
            stgA_last[par] = P.copy(wAq(wi, ci), stg, waits=[tl])
    t_bvl = P.dma(wst[0:1, :], bvr_d, waits=[t for t in stgA_last if t])
    for h in range(2):
        tmz = P.mm(pp[h][:, 0:512], o1s[:], wst[0:1, 512 * h:512 * h + 512],
                   start=True, stop=True, waits=[t_bvl])
        P.act(bvr[:, 512 * h:512 * h + 512], pp[h][:, 0:512], AF.Copy, waits=[tmz])
    P.barrier()

    slabs768 = [(0, 512), (512, 256)]
    for i in range(4):
        t_xc = []
        for ci in range(NCIN):
            tl = pool_x.dma(xt[:, WID * ci:WID * ci + WID],
                            xa[i, 128 * ci:128 * ci + 128, :])
            t_xc.append(P.copy(xtb[:, WID * ci:WID * ci + WID],
                               xt[:, WID * ci:WID * ci + WID], waits=[tl]))
        ppi = 0
        ppb = [B0, B1, B2, B3]
        pp_free = [None, None, None, None]
        for wi, outb in ((0, qt), (1, kt)):
            for do in range(NCIN):
                for (s0, sw) in slabs768:
                    bank = ppb[ppi % 4]
                    waits = list(t_xc)
                    if pp_free[ppi % 4] is not None:
                        waits.append(pp_free[ppi % 4])
                    pairs = [(wAq(wi, ci)[:, 128 * do:128 * do + 128],
                              xtb[:, WID * ci + s0:WID * ci + s0 + sw])
                             for ci in range(NCIN)]
                    tmm = P.mmchain(bank[:, 0:sw], pairs, waits=waits)
                    pp_free[ppi % 4] = P.act(
                        outb[:, WID * do + s0:WID * do + s0 + sw], bank[:, 0:sw],
                        AF.Identity, bias=b3s[:, 8 * wi + do:8 * wi + do + 1],
                        waits=[tmm])
                    ppi += 1
        for tti in range(6):
            for half in range(2):
                bank = ppb[ppi % 4]
                waits = list(t_xc)
                if pp_free[ppi % 4] is not None:
                    waits.append(pp_free[ppi % 4])
                pairs = [(xtb[:, WID * ci + 128 * tti:WID * ci + 128 * tti + 128],
                          wAq(2, ci)[:, 512 * half:512 * half + 512])
                         for ci in range(NCIN)]
                tmm = P.mmchain(bank[:, 0:512], pairs, waits=waits)
                pp_free[ppi % 4] = P.tt(
                    vt[:, 1024 * tti + 512 * half:1024 * tti + 512 * half + 512],
                    bank[:, 0:512], bvr[:, 512 * half:512 * half + 512],
                    ALU.add, waits=[tmm])
                ppi += 1
        P.barrier()

        mmt = {}
        red_prev = None
        sq_free = {}
        for ti, src in enumerate((qt, kt)):
            for isl, (s0, sw) in enumerate(slabs768):
                last = None
                for ci in range(NCIN):
                    k2 = ci % 2
                    sqk = sqm[:, 512 * k2:512 * k2 + sw]
                    w = [sq_free[k2]] if k2 in sq_free else []
                    t1 = P.act(sqk, src[:, WID * ci + s0:WID * ci + s0 + sw],
                               AF.Square, waits=w)
                    wmm = [t1] + ([red_prev] if (ci == 0 and red_prev) else [])
                    last = P.mm(pz[0:1, 0:sw], obs[:], sqk, start=(ci == 0),
                                stop=(ci == NCIN - 1), waits=wmm)
                    sq_free[k2] = last
                mmt[(ti, isl)] = P.emit("vector", lambda e, sw=sw, ti=ti, isl=isl: e.tensor_reduce(
                    out=mrow[0:1, 3 * ti + isl:3 * ti + isl + 1], in_=pz[0:1, 0:sw],
                    axis=mybir.AxisListType.X, op=ALU.max), waits=[last])
                red_prev = mmt[(ti, isl)]
        t1 = P.tt(mrow[0:1, 6:7], mrow[0:1, 0:1], mrow[0:1, 1:2], ALU.max,
                  waits=list(mmt.values()))
        t2 = P.tt(mrow[0:1, 7:8], mrow[0:1, 3:4], mrow[0:1, 4:5], ALU.max,
                  waits=list(mmt.values()))
        t3 = P.tt(mrow[0:1, 6:7], mrow[0:1, 6:7], mrow[0:1, 7:8], ALU.mult,
                  waits=[t1, t2])
        t4 = P.act(mrow[0:1, 6:7], mrow[0:1, 6:7], AF.Sqrt, waits=[t3])
        t5 = P.ts(mrow[0:1, 6:7], mrow[0:1, 6:7], -INV32, ALU.mult, waits=[t4])
        t6 = P.mm(pzb[:, 0:128][:, 0:1], o1s[:], mrow[0:1, 6:7], start=True,
                  stop=True, waits=[t5])
        P.act(biasM[:], pzb[:, 0:128][:, 0:1], AF.Copy, waits=[t6])
        P.barrier()

        ss_free = expm_free = po_free = pzb_free = osb_free = None
        tzr_prev = tzb_prev = None
        mk_free = {}
        tz0 = P.memset(expm[:, 0:128], 0.0)
        for j in range(-1, 4):
            qa = 128 * (j + 1)
            msl = (j + 1) % 2
            wmk = [mk_free[msl]] if msl in mk_free else []
            tmkl = P.dma(mka[:, 384 * msl:384 * msl + 384], mk_d[i, j + 1],
                         waits=wmk, eng="gpsimd")
            smm = []
            p0 = 0 if j >= 0 else 1
            for p in range(p0, 3):
                kcol = qa + 128 * (p - 1)
                w0 = [ss_free] if ss_free else []
                pairs = [(kt[:, WID * ci + kcol:WID * ci + kcol + 128],
                          qt[:, WID * ci + qa:WID * ci + qa + 128])
                         for ci in range(NCIN)]
                smm.append(P.mmchain(B3[:, 128 * p:128 * p + 128], pairs, waits=w0))
            we = list(smm) + ([expm_free] if expm_free else []) + [tz0]
            te = P.act(expm[:, 128 * p0:384], B3[:, 128 * p0:384], AF.Exp,
                       bias=biasM[:], scale=INV32, waits=we)
            ss_free = te
            tmsk = P.tt(expm[:, 0:384], expm[:, 0:384],
                        mka[:, 384 * msl:384 * msl + 384], ALU.mult,
                        waits=[te, tmkl])
            mk_free[msl] = tmsk
            zmm = None
            for p in range(3):
                w0 = []
                if p == 0:
                    w0 = [tmsk] + ([tzr_prev] if tzr_prev else [])
                zmm = P.mm(pz[0:1, 0:128], obs[:], expm[:, 128 * p:128 * p + 128],
                           start=(p == 0), stop=(p == 2), waits=w0)
            tzr = P.recip(zrA[:], pz[0:1, 0:128],
                          waits=[zmm] + ([tzb_prev] if tzb_prev else []))
            tzr_prev = tzr
            tzb = P.mm(pzb[:, 0:128], o1s[:], zrA[:], start=True, stop=True,
                       waits=[tzr] + ([pzb_free] if pzb_free else []))
            tzb_prev = tzb
            pvt = None
            p_list = [p for p in range(3) if (j + p) >= 0]
            for co in range(8):
                for ip, p in enumerate(p_list):
                    tti = j + p
                    bank = B4 if co < 4 else B5
                    cof = co % 4
                    w0 = []
                    if co == 0 and ip == 0:
                        w0 = [tmsk] + ([po_free] if po_free else [])
                    pvt = P.mm(bank[:, 128 * cof:128 * cof + 128],
                               vt[:, 1024 * tti + 128 * co:1024 * tti + 128 * co + 128],
                               expm[:, 128 * p:128 * p + 128],
                               start=(ip == 0), stop=(ip == len(p_list) - 1), waits=w0)
            expm_free = pvt
            wev = [pvt] + ([osb_free] if osb_free else [])
            tev1 = P.act(o_sb[:, 0:512], B4[:], AF.Copy, waits=wev)
            tev2 = P.act(o_sb[:, 512:1024], B5[:], AF.Copy, waits=wev)
            po_free = tev2
            tmu = P.tt(r3(o_sb, 8, 128), r3(o_sb, 8, 128), bc3(pzb[:, 0:128], 8),
                       ALU.mult, waits=[tev1, tev2, tzb])
            pzb_free = tmu
            xv = r3(xt, NCIN, WID)[:, :, qa:qa + 128]
            tres = P.tt(xv, xv, r3(o_sb, 8, 128), ALU.add, waits=[tmu])
            osb_free = tres
        P.barrier()

        def dst_x1(oc, sw, t12, i=i):
            td = None
            for ci in range(NCIN):
                td = P.dma(x1_d[i, 128 * ci:128 * ci + 128, oc:oc + sw],
                           t_sbA[:, 512 * ci:512 * ci + sw], waits=[t12], eng="gpsimd")
            return td
        _ln_fm(P, T, xt, WID, [(127, 512, 0), (128, 512, 1)], dst_x1, {})
        P.barrier()
    P.barrier()

    x1B = at_("x1B", 0, [128, NCIN * XW], F32)
    dxt = at_("dxtB", 16416, [128, 4096], F32)
    xrb = at_("xrbB", 32800, [128, 4096], BF16)
    xkb = at_("xkbB", 40992, [128, 4096], BF16)
    wB = at_("wB", 49184, [128, 3 * NCIN * 1024], BF16)
    r_bf = at_("rbfB", 98336, [128, 4096], BF16)
    ka_bf = at_("kabB", 106528, [128, 4096], BF16)
    vsbB = at_("vsbB", 114720, [128, 512], F32)
    x2B = at_("x2B", 116768, [128, 4096], F32)
    lnrbB = at_("lnrbB", 133152, [128, 4096], BF16)
    lnsqB = at_("lnsqB", 141344, [128, 4096], BF16)
    wstB = at_("wstB", 149536, [128, 1024], F32)
    wstB2 = at_("wstB2", 153632, [128, 1024], F32)
    wBq = lambda wi, ci: wB[:, 1024 * (NCIN * wi + ci):1024 * (NCIN * wi + ci) + 1024]

    stgB_last = [None, None]
    cntB = 0
    for wi, wd in enumerate((wr_d, wk2_d, wv2_d)):
        for ci in range(NCIN):
            par = cntB % 2
            cntB += 1
            stg = (wstB, wstB2)[par]
            tl = pool_w.dma(stg, wd[128 * ci:128 * ci + 128, :],
                            waits=([stgB_last[par]] if stgB_last[par] else []))
            stgB_last[par] = P.copy(wBq(wi, ci), stg, waits=[tl])
    P.barrier()
    T["lnrb"] = lnrbB
    T["lnsq"] = lnsqB
    T["lnout"] = x2B

    for i in range(4):
        t_xl = [pool_x.dma(x1B[:, XW * ci:XW * ci + XW], x1_d[i, 128 * ci:128 * ci + 128, :])
                for ci in range(NCIN)]
        t_pm = P.dma(wstB[:, 0:1], pm_d[i], waits=[t_xl[-1]])
        c0v = r3(x1B, NCIN, XW)[:, :, 0:1]
        t_z0 = P.ts(c0v, c0v, wstB[:, 0:1], ALU.mult, waits=[t_pm] + t_xl)
        t_mix = []
        for ci in range(NCIN):
            xc = x1B[:, XW * ci + 1:XW * ci + XW]
            xp = x1B[:, XW * ci:XW * ci + XW - 1]
            td = P.tt(dxt[:, 512 * ci:512 * ci + 512], xc, xp, ALU.subtract,
                      waits=[t_z0])
            t1m = P.stt(xrb[:, 512 * ci:512 * ci + 512], dxt[:, 512 * ci:512 * ci + 512],
                        mus[:, ci:ci + 1], xp, ALU.mult, ALU.add, waits=[td])
            t2m = P.stt(xkb[:, 512 * ci:512 * ci + 512], dxt[:, 512 * ci:512 * ci + 512],
                        mus[:, 8 + ci:8 + ci + 1], xp, ALU.mult, ALU.add, waits=[td])
            t_mix += [t1m, t2m]
        ppi = 0
        ppb = [B0, B1, B2, B3]
        pp_free = [None, None, None, None]
        proj_last = None
        for wi, (inb, outb, fn) in enumerate(((xrb, r_bf, AF.Sigmoid),
                                              (xkb, ka_bf, AF.Relu))):
            for do in range(NCIN):
                bank = ppb[ppi % 4]
                waits = list(t_mix)
                if pp_free[ppi % 4] is not None:
                    waits.append(pp_free[ppi % 4])
                pairs = [(wBq(wi, ci)[:, 128 * do:128 * do + 128],
                          inb[:, 512 * ci:512 * ci + 512]) for ci in range(NCIN)]
                tmm = P.mmchain(bank[:, 0:512], pairs, waits=waits)
                te = P.act(outb[:, 512 * do:512 * do + 512], bank[:, 0:512], fn,
                           waits=[tmm])
                pp_free[ppi % 4] = te
                proj_last = te
                ppi += 1
        tsq2 = P.act(ka_bf, ka_bf, AF.Square, waits=[proj_last])
        vfree = None
        for do in range(NCIN):
            bank = ppb[ppi % 4]
            waits = [tsq2]
            if pp_free[ppi % 4] is not None:
                waits.append(pp_free[ppi % 4])
            pairs = [(wBq(2, ci)[:, 128 * do:128 * do + 128],
                      ka_bf[:, 512 * ci:512 * ci + 512]) for ci in range(NCIN)]
            tmm = P.mmchain(bank[:, 0:512], pairs, waits=waits)
            wv_ = [tmm] + ([vfree] if vfree else [])
            tev = P.act(vsbB, bank[:, 0:512], AF.Copy, waits=wv_)
            pp_free[ppi % 4] = tev
            tm2 = P.tt(vsbB, vsbB, r_bf[:, 512 * do:512 * do + 512], ALU.mult,
                       waits=[tev])
            xs_ = x1B[:, XW * do + 1:XW * do + 1 + 512]
            tr2 = P.tt(xs_, xs_, vsbB, ALU.add, waits=[tm2])
            vfree = tr2
            ppi += 1
        P.barrier()

        def dst_x2(oc, sw, t12, i=i):
            td = None
            for ci in range(NCIN):
                td = P.dma(x2_d[i, 128 * ci:128 * ci + 128, oc:oc + sw],
                           x2B[:, 512 * ci:512 * ci + sw], waits=[t12], eng="gpsimd")
            return td
        _ln_fm(P, T, x1B, XW, [(1, 512, 0)], dst_x2, {})
        P.barrier()
    P.barrier()

    x2C = at_("x2C", 0, [128, 4096], F32)
    x2b = at_("x2bC", 16384, [128, 4096], BF16)
    wqkv = at_("wqkvC", 24576, [128, 3072], BF16)
    wstC = at_("wstC", 30720, [128, 1024], F32)
    cssC = at_("cssC", 34816, [32, 512], F32)
    snsC = at_("snsC", 36864, [32, 512], F32)
    qfC = at_("qfC", 38912, [128, 512], F32)
    kfC = at_("kfC", 40960, [128, 512], F32)
    raC = at_("raC", 43008, [32, 512], F32)
    rbC = at_("rbC", 45056, [32, 512], F32)
    qoddC = at_("qoddC", 47104, [32, 512], F32)
    wstC2 = at_("wstC2", 57344, [128, 1024], F32)
    re_tC = at_("re_tC", 49152, [32, 512], BF16)
    ro_tC = at_("ro_tC", 50176, [32, 512], BF16)
    skpC = at_("skpC", 51200, [128, 512], BF16)
    vtileC = at_("vtileC", 52224, [128, 128], BF16)
    sqC = at_("sqC", 52480, [128, 512], BF16)
    knmC = at_("knmC", 53504, [1, 32], F32)

    stgC_cnt = 0
    stgC_last = [None, None]
    for i in range(4):
        t_x2l = []
        for ci in range(NCIN):
            tl = pool_x.dma(x2C[:, 512 * ci:512 * ci + 512],
                            x2_d[i, 128 * ci:128 * ci + 128, :])
            t_x2l.append(tl)
        t_x2b = P.copy(x2b, x2C, waits=t_x2l)
        t_cs = P.dma(cssC[:], cs_d[i])
        t_sn = P.dma(snsC[:], sn_d[i])
        prev = {}
        for h in range(NH):
            tags_w = []
            for wi, wd in enumerate((wq1_d, wk1_d, wv1_d)):
                par = stgC_cnt % 2
                stgC_cnt += 1
                stg = (wstC, wstC2)[par]
                tl = pool_w.dma(r3(stg, NCIN, 128),
                                wd[h].rearrange("(c p) d -> p c d", p=128),
                                waits=([stgC_last[par]] if stgC_last[par] else []))
                wcp = [tl]
                if wi == 0 and prev.get("wq"):
                    wcp.append(prev["wq"])
                tcw = P.copy(wqkv[:, 1024 * wi:1024 * wi + 1024], stg, waits=wcp)
                stgC_last[par] = tcw
                tags_w.append(tcw)
            for ti, ff in ((0, qfC), (1, kfC)):
                bank = pp[ti]
                w0 = [t_x2b] + tags_w + ([prev[("pp", ti)]] if ("pp", ti) in prev else [])
                pairs = [(wqkv[:, 1024 * ti + 128 * ci:1024 * ti + 128 * ci + 128],
                          x2b[:, 512 * ci:512 * ci + 512]) for ci in range(NCIN)]
                tmm = P.mmchain(bank[:, 0:512], pairs, waits=w0)
                wtv = [tmm] + [prev[k] for k in (("mv", ti), ("ff", ti)) if k in prev]
                tev = P.act(ff, bank[:, 0:512], AF.Copy, waits=wtv)
                prev[("pp", ti)] = tev
                wo_ = [prev[("o",)]] if ("o",) in prev else []
                wrp = [prev[("rope",)]] if ("rope",) in prev else []
                wod = [prev[("odd",)]] if ("odd",) in prev else []
                tmv = P.dma(qoddC, ff[32:64, :], waits=[tev] + wod)
                a1 = P.tt(raC, ff[0:32, :], cssC[:], ALU.mult,
                          waits=[tev, t_cs, t_sn] + wo_ + wrp)
                a2 = P.tt(rbC, qoddC, snsC[:], ALU.mult,
                          waits=[tev, tmv, t_cs, t_sn] + wo_ + wrp)
                r1 = P.tt(re_tC, raC, rbC, ALU.subtract,
                          waits=[a1, a2] + ([prev[("d1",)]] if ("d1",) in prev else []))
                a3 = P.tt(raC, ff[0:32, :], snsC[:], ALU.mult, waits=[r1])
                a4 = P.tt(rbC, qoddC, cssC[:], ALU.mult, waits=[r1])
                r2 = P.tt(ro_tC, raC, rbC, ALU.add,
                          waits=[a3, a4] + ([prev[("d2",)]] if ("d2",) in prev else []))
                r3c = P.copy(skpC[64:128, :], ff[64:128, :], waits=[tev] + wo_)
                if ti == 0:
                    d1 = P.dma(qo_d[i, h, 0:32, :], re_tC, waits=[r1], eng="gpsimd")
                    d2 = P.dma(qo_d[i, h, 32:64, :], ro_tC, waits=[r2], eng="gpsimd")
                    d3 = P.dma(qo_d[i, h, 64:128, :], skpC[64:128, :], waits=[r3c], eng="gpsimd")
                else:
                    kv0 = kvin.ap()
                    d1 = P.dma(kv0[0, h, i, 0:32, :], re_tC, waits=[r1], eng="gpsimd")
                    d2 = P.dma(kv0[0, h, i, 32:64, :], ro_tC, waits=[r2], eng="gpsimd")
                    d3 = P.dma(kv0[0, h, i, 64:128, :], skpC[64:128, :], waits=[r3c], eng="gpsimd")
                prev[("o",)] = d3
                prev[("d1",)] = d1
                prev[("d2",)] = d2
                prev[("mv", ti)] = tmv
                prev[("odd",)] = a4
                prev[("rope",)] = r2
                prev[("ff", ti)] = P.esem["vector"].tag()
                if ti == 1:
                    wn = [tev] + ([prev[("sqk",)]] if ("sqk",) in prev else [])
                    tq1 = P.act(sqC, ff, AF.Square, waits=wn)
                    nmm = P.mm(pz[0:1, 0:512], obs[:], sqC, start=True, stop=True,
                               waits=[tq1] + ([prev[("nred",)]] if ("nred",) in prev else []))
                    prev[("sqk",)] = nmm
                    tnr = P.emit("vector", lambda e, h=h: e.tensor_reduce(
                        out=knmC[0:1, h:h + 1], in_=pz[0:1, 0:512],
                        axis=mybir.AxisListType.X, op=ALU.max),
                        waits=[nmm] + ([prev[("nuse",)]] if ("nuse",) in prev else []))
                    prev[("nred",)] = tnr
            for tt_ in range(4):
                bank = B3
                w0 = [t_x2b] + tags_w + ([prev[("pv",)]] if ("pv",) in prev else [])
                pairs = [(x2b[:, 512 * ci + 128 * tt_:512 * ci + 128 * tt_ + 128],
                          wqkv[:, 2048 + 128 * ci:2048 + 128 * ci + 128])
                         for ci in range(NCIN)]
                tmm = P.mmchain(bank[:, 0:128], pairs, waits=w0)
                wv2 = [tmm] + ([prev[("vo",)]] if ("vo",) in prev else [])
                tev = P.act(vtileC, bank[:, 0:128], AF.Copy, waits=wv2)
                prev[("pv",)] = tev
                dv = P.dma(kvin.ap()[1, h, i, :, 128 * tt_:128 * tt_ + 128],
                           vtileC, waits=[tev], eng="gpsimd")
                prev[("vo",)] = dv
            prev["wq"] = P.esem["tensor"].tag()
        tcp2 = P.copy(mrow[0:1, 16 + 8 * i:24 + 8 * i], knmC[0:1, 0:8],
                      waits=[prev[("nred",)]] + ([prev.get(("nuse2",))] if ("nuse2",) in prev else []))
        prev[("nuse",)] = tcp2
        P.barrier()
    tnr1 = P.copy(nr4[0:1, 0:32], mrow[0:1, 16:48])
    tnr2 = P.dma(nrin.ap()[0:1, :], nr4[0:1, 0:32], waits=[tnr1], eng="gpsimd")
    P.barrier()

    ccs1 = P.sem("cc1")
    ccs1b = P.sem("cc1b")
    def cc1(e, s=ccs1, s2=ccs1b):
        e.collective_compute(
            "AllGather", mybir.AluOpType.bypass,
            replica_groups=[[0, 2, 4, 6], [1, 3, 5, 7]],
            ins=[kvin.ap().opt()], outs=[kvg.ap().opt()],
        ).then_inc(s.h, 1)
        e.collective_compute(
            "AllGather", mybir.AluOpType.bypass,
            replica_groups=[[0, 2, 4, 6], [1, 3, 5, 7]],
            ins=[nrin.ap().opt()], outs=[nrg.ap().opt()],
        ).then_inc(s2.h, 1)
    P.raw("gpsimd", cc1)
    ccs1.n += 1
    ccs1b.n += 1
    t_cc1 = (ccs1, 1)
    t_cc1b = (ccs1b, 1)
    P.barrier()

    qbP = at_("qbP", 0, [128, 512], BF16)
    kbP = at_("kbP", 1024, [128, 7680], BF16)
    vbP = at_("vbP", 16384, [128, 7680], BF16)
    dkbP = at_("dkbP", 31744, [128, 512], BF16)
    dvbP = at_("dvbP", 32768, [128, 512], BF16)
    atP = at_("atP", 33792, [128, 4096], BF16)
    tmP = at_("tmP", 41984, [128, 64], F32)
    baP = at_("baP", 42240, [128, 64], F32)
    expbP = at_("expbP", 42496, [128, 2048], BF16)
    zrP = at_("zrP", 120128, [1, 512], F32)
    osbP = at_("osbP", 46592, [128, 512], F32)
    woBP = at_("woBP", 48640, [128, 8192], BF16)
    x2P = at_("x2P", 65024, [128, 4096], F32)
    x3P = at_("x3P", 81408, [128, 4096], F32)
    lnrbP = at_("lnrbP", 97792, [128, 4096], BF16)
    lnsqP = at_("lnsqP", 105984, [128, 4096], BF16)
    wstP = at_("wstP", 114176, [128, 1024], F32)
    sqP = at_("sqP", 118272, [128, 512], BF16)
    bMh = at_("bMh", 119296, [128, 8], F32)
    trtP = at_("trtP", 119360, [128, 128], F32)
    trtPb = at_("trtPb", 119872, [128, 128], BF16)
    idIs = idIsp[:]
    T["lnrb"] = lnrbP
    T["lnsq"] = lnsqP
    T["lnout"] = x3P

    t_idI = P.dma(idIs, idI_d)
    t_n4 = P.dma(nr4[0:1, :], nrg.ap().rearrange("r k -> (r k)").unsqueeze(0), waits=[t_cc1b])
    t_mk2 = P.emit("vector", lambda e: e.tensor_reduce(
        out=maxk2[0:1, :],
        in_=nr4[0:1, :].rearrange("p (r i h) -> p h r i", r=4, i=4, h=8),
        axis=mybir.AxisListType.XY, op=ALU.max), waits=[t_n4])
    wstP2 = at_("wstP2", 122176, [128, 1024], F32)
    stgP_last = [None, None]
    for ci in range(NCIN):
        par = ci % 2
        stg = (wstP, wstP2)[par]
        tl = pool_w.dma(stg, wo_d[128 * ci:128 * ci + 128, :],
                        waits=([stgP_last[par]] if stgP_last[par] else []))
        stgP_last[par] = P.copy(woBP[:, 1024 * ci:1024 * ci + 1024], stg, waits=[tl])
    P.barrier()

    for i in range(4):
        Pi = PEXT[i]
        nkr = Pi - 4
        t_tm = P.dma(tmP, tm_d[i])
        t_x2 = [pool_x.dma(x2P[:, 512 * ci:512 * ci + 512],
                           x2_d[i, 128 * ci:128 * ci + 128, :]) for ci in range(NCIN)]
        prev = {}
        at_done = []
        for h in range(NH):
            t_q = P.dma(qbP, qo_d[i, h], waits=[t for t in (prev.get("qfree"),) if t])
            wq1_ = [t_q] + [t for t in (prev.get("sqP"),) if t]
            tq1 = P.act(sqP, qbP, AF.Square, waits=wq1_)
            nmm = P.mm(pz[0:1, 0:512], obs[:], sqP, start=True, stop=True,
                       waits=[tq1] + [t for t in (prev.get("pzR"),) if t])
            prev["sqP"] = nmm
            tnr = P.emit("vector", lambda e: e.tensor_reduce(
                out=mrow[0:1, 0:1], in_=pz[0:1, 0:512],
                axis=mybir.AxisListType.X, op=ALU.max),
                waits=[nmm] + [t for t in (prev.get("mr0"),) if t])
            tb1 = P.tt(mrow[0:1, 1:2], mrow[0:1, 0:1], maxk2[0:1, h:h + 1], ALU.mult,
                       waits=[tnr, t_mk2] + [t for t in (prev.get("mr1"),) if t])
            tb2 = P.act(mrow[0:1, 1:2], mrow[0:1, 1:2], AF.Sqrt, waits=[tb1])
            tb3 = P.ts(mrow[0:1, 1:2], mrow[0:1, 1:2], -INV128, ALU.mult, waits=[tb2])
            prev["mr0"] = tb3
            tb4 = P.mm(pzb[:, 0:128][:, 0:1], o1s[:], mrow[0:1, 1:2], start=True,
                       stop=True, waits=[tb3] + [t for t in (prev.get("zbuse"),) if t])
            tb5 = P.act(bMh[:, 0:1], pzb[:, 0:128][:, 0:1], AF.Copy,
                        waits=[tb4] + [t for t in (prev.get("bMuse"),) if t])
            prev["mr1"] = tb5
            tba = P.ts(baP, tmP, bMh[:, 0:1], ALU.add,
                       waits=[tb5, t_tm] + [t for t in (prev.get("bause"),) if t])
            prev["bMuse"] = tba
            t_kc = {}
            t_vc = {}
            wkv = [t_cc1] + [t for t in (prev.get("kvfree"),) if t]
            for g in range(nkr // 4):
                t_kc[g] = pool_kv.dma(kbP[:, 512 * g:512 * g + 512],
                                kvg.ap()[g % 4, 0, h, g // 4], waits=wkv)
                t_vc[g] = pool_kv.dma(vbP[:, 512 * g:512 * g + 512],
                                kvg.ap()[g % 4, 1, h, g // 4], waits=wkv)
            t_dk = pool_kv.dma(dkbP, kvin.ap()[0, h, i], waits=wkv)
            t_dv = pool_kv.dma(dvbP, kvin.ap()[1, h, i], waits=wkv)
            t_exp = {}
            pvt = zmm = None
            sbanks = [pp[0], pp[1], B2, B5]
            nit = nkr + 4
            for kt in range(nit):
                sl = kt % 4
                diag = kt >= nkr
                jt = kt - nkr
                if diag:
                    lhs_s = dkbP[:, 128 * jt:128 * jt + 128]
                    lhs_v = dvbP[:, 128 * jt:128 * jt + 128]
                    wsrc = [t_dk]
                    wvsrc = [t_dv]
                else:
                    lhs_s = kbP[:, 128 * kt:128 * kt + 128]
                    lhs_v = vbP[:, 128 * kt:128 * kt + 128]
                    wsrc = [t_kc[kt // 4]]
                    wvsrc = [t_vc[kt // 4]]
                w0 = wsrc + [t_q, tba] + ([t_exp[sl]] if sl in t_exp else [])
                tsm = P.mm(sbanks[sl][:], lhs_s, qbP, start=True, stop=True, waits=w0)
                we = [tsm, tba]
                if kt >= 4:
                    we.append(prev.get(("euse", sl)))
                te = P.act(expbP[:, 512 * sl:512 * sl + 512], sbanks[sl][:], AF.Exp,
                           bias=(bMh[:, 0:1] if diag else baP[:, kt:kt + 1]),
                           scale=INV128, waits=[w for w in we if w])
                tin = te
                if diag:
                    tin = P.tt(expbP[:, 512 * sl:512 * sl + 512],
                               expbP[:, 512 * sl:512 * sl + 512],
                               cms[:, 512 * jt:512 * jt + 512], ALU.mult, waits=[te])
                t_exp[sl] = tin
                wp = [tin] + wvsrc + ([prev.get("poB3")] if (kt == 0 and "poB3" in prev) else [])
                pvt = P.mm(B3[:], lhs_v, expbP[:, 512 * sl:512 * sl + 512],
                           start=(kt == 0), stop=(kt == nit - 1),
                           waits=[w for w in wp if w])
                wz = [tin] + ([prev.get("pzR")] if (kt == 0 and "pzR" in prev) else [])
                zmm = P.mm(pz[0:1, 0:512], obs[:], expbP[:, 512 * sl:512 * sl + 512],
                           start=(kt == 0), stop=(kt == nit - 1),
                           waits=[w for w in wz if w])
                prev[("euse", sl)] = zmm
            tzr = P.recip(zrP[:], pz[0:1, 0:512],
                          waits=[zmm] + [t for t in (prev.get("zruse"),) if t])
            prev["pzR"] = tzr
            tzb = P.mm(B4[:], o1s[:], zrP[:], start=True, stop=True,
                       waits=[tzr] + [t for t in (prev.get("B4use"),) if t])
            prev["zruse"] = tzb
            tev = P.act(osbP, B3[:], AF.Copy,
                        waits=[pvt] + [t for t in (prev.get("osb"),) if t])
            prev["poB3"] = tev
            tmu = P.tt(atP[:, 512 * h:512 * h + 512], osbP, B4[:], ALU.mult,
                       waits=[tev, tzb] + [t for t in (prev.get("atuse"),) if t])
            prev["B4use"] = tmu
            prev["osb"] = tmu
            prev["zbuse"] = tmu
            prev["bause"] = P.esem["scalar"].tag()
            prev["qfree"] = P.esem["tensor"].tag()
            prev["kvfree"] = prev["qfree"]
            at_done.append(tmu)
        vfree = None
        pp_free = [None, None]
        for do in range(NCIN):
            bank = pp[do % 2]
            waits = list(at_done)
            if pp_free[do % 2] is not None:
                waits.append(pp_free[do % 2])
            pairs = [(woBP[:, 1024 * h + 128 * do:1024 * h + 128 * do + 128],
                      atP[:, 512 * h:512 * h + 512]) for h in range(NH)]
            tmm = P.mmchain(bank[:, 0:512], pairs, waits=waits)
            wv_ = [tmm] + ([vfree] if vfree else [])
            tev = P.act(osbP, bank[:, 0:512], AF.Copy, waits=wv_)
            pp_free[do % 2] = tev
            xs_ = x2P[:, 512 * do:512 * do + 512]
            tr2 = P.tt(xs_, xs_, osbP, ALU.add, waits=[tev] + t_x2)
            vfree = tr2
        P.barrier()

        def dst_x3(oc, sw, t12, i=i):
            td = None
            for ci in range(NCIN):
                td = P.dma(x3_d[i, 128 * ci:128 * ci + 128, oc:oc + sw],
                           x3P[:, 512 * ci:512 * ci + sw], waits=[t12], eng="gpsimd")
            return td
        _ln_fm(P, T, x2P, 512, [(0, 512, 0)], dst_x3, {})
        P.barrier()
        pu = {}
        for ci in range(NCIN):
            for tq in range(4):
                w0 = [t_idI] + [t for t in (pu.get("ps"),) if t]
                ttr = P.transpose(B5[:, 0:128],
                                  x3P[:, 512 * ci + 128 * tq:512 * ci + 128 * tq + 128],
                                  idIs, waits=w0)
                tevt = P.act(trtP, B5[:, 0:128], AF.Copy,
                             waits=[ttr] + [t for t in (pu.get("tr"), pu.get("cb")) if t])
                pu["ps"] = tevt
                tcb = P.copy(trtPb, trtP, waits=[tevt] + [t for t in (pu.get("trb"),) if t])
                pu["cb"] = tcb
                r0_ = 512 * i + 128 * tq
                tdt = P.dma(x3r_d.ap()[r0_:r0_ + 128, 128 * ci:128 * ci + 128],
                            trtP, waits=[tevt], eng="gpsimd")
                tdb = P.dma(x3rb_d.ap()[r0_:r0_ + 128, 128 * ci:128 * ci + 128],
                            trtPb, waits=[tcb], eng="gpsimd")
                pu["tr"] = tdt
                pu["trb"] = tdb
        P.barrier()
    P.barrier()

    x3bM = at_("x3bM", 0, [128, 8192], BF16)
    accM = at_("accM", 16384, [128, 8192], F32)
    htM = at_("htM", 49152, [128, 32768], BF16)
    stgM = [at_("stgM0", 114688, [128, 1024], F32),
            at_("stgM1", 118784, [128, 1024], F32),
            at_("stgM2", 78016, [128, 1024], F32),
            at_("stgM3", 82112, [128, 1024], F32)]
    w1bM = [at_("w1bM0", 122880, [128, 1024], BF16),
            at_("w1bM1", 124928, [128, 1024], BF16),
            at_("w1bM2", 86208, [128, 1024], BF16),
            at_("w1bM3", 88256, [128, 1024], BF16)]
    w2bM = [at_("w2bM0", 126976, [128, 512], BF16),
            at_("w2bM1", 128000, [128, 512], BF16),
            at_("w2bM2", 90304, [128, 512], BF16),
            at_("w2bM3", 91328, [128, 512], BF16)]
    ygs2 = [at_("ygs0", 63488, [128, 512], F32),
            at_("ygs1", 92352, [128, 512], F32)]
    scfsM = at_("scfsM", 129024, [128, 512], F32)
    mskM = at_("mskM", 131072, [128, 512], BF16)
    rowsumM = at_("rowsumM", 132096, [128, 1], F32)
    taubM = at_("taubM", 132128, [128, 1], F32)
    b1sM = at_("b1sM", 132160, [128, 256], F32)
    b2sM = at_("b2sM", 133184, [8, 1024], F32)
    sqtM = at_("sqtM", 137280, [128, 1024], F32)
    x4tM = at_("x4tM", 141376, [128, 1024], F32)
    mrow4 = at_("mrow4", 145472, [128, 4], F32)
    rwbM = at_("rwbM", 145504, [128, 64], BF16)
    lgM = at_("lgM", 145664, [8, 512], F32)
    ltmM = at_("ltmM", 147712, [128, 8], F32)
    mxM = at_("mxM", 147744, [128, 1], F32)
    smM = at_("smM", 147776, [128, 1], F32)
    sctM = at_("sctM", 147808, [128, 8], F32)
    scTM = at_("scTM", 147840, [8, 512], F32)

    t_b1l = P.dma(b1sM, b1_d)
    t_b2l = P.dma(b2sM[:, :], b2_d)
    t_rw = P.dma(r3(wstP[:, 0:64], NCIN, 8), rw_d.rearrange("(c p) e -> p c e", p=128))
    t_rwb = P.copy(rwbM[:, 0:64], wstP[:, 0:64], waits=[t_rw])
    pu = {}
    sc_dma = []
    for i in range(4):
        t_xl = []
        for ci in range(NCIN):
            k = ci % 2
            w0 = [t_xl[ci - 2]] if ci >= 2 else [t_rwb] + [t for t in (pu.get("stg"),) if t]
            tl = P.dma(stgM[k][:, 0:512], x3_d[i, 128 * ci:128 * ci + 128, :], waits=w0)
            t_xl.append(P.copy(x3bM[:, 512 * ci:512 * ci + 512], stgM[k][:, 0:512],
                               waits=[tl] + [t for t in (pu.get("x3b"),) if t]))
        pu["stg"] = t_xl[-1]
        ev_prev = pu.get("lg")
        pairs = [(rwbM[:, 8 * ci:8 * ci + 8], x3bM[:, 512 * ci:512 * ci + 512])
                 for ci in range(NCIN)]
        tmm = P.mmchain(B3[0:8, :], pairs,
                        waits=t_xl + [t_rwb] + ([ev_prev] if ev_prev else []))
        tevl = P.act(lgM[:, :], B3[0:8, :], AF.Copy, waits=[tmm])
        pu["lg"] = tevl
        pu["x3b"] = tmm
        for tt in range(4):
            w0 = [tevl] + [t for t in (pu.get("pt"),) if t]
            ttr = P.transpose(B5[0:128, 0:8], lgM[:, 128 * tt:128 * tt + 128],
                              id8s[:], waits=w0)
            tevt = P.act(ltmM, B5[0:128, 0:8], AF.Copy,
                         waits=[ttr] + [t for t in (pu.get("ltm"),) if t])
            pu["pt"] = tevt
            tmx = P.emit("vector", lambda e: e.tensor_reduce(
                out=mxM[:], in_=ltmM, axis=mybir.AxisListType.X, op=ALU.max),
                waits=[tevt] + [t for t in (pu.get("mx"),) if t])
            tsb = P.ts(sctM, ltmM, mxM[:, 0:1], ALU.subtract,
                       waits=[tmx] + [t for t in (pu.get("sct"),) if t])
            pu["mx"] = tsb
            pu["ltm"] = tsb
            tex = P.act(sctM, sctM, AF.Exp, waits=[tsb])
            tsm = P.emit("vector", lambda e: e.tensor_reduce(
                out=smM[:], in_=sctM, axis=mybir.AxisListType.X, op=ALU.add),
                waits=[tex] + [t for t in (pu.get("sm"),) if t])
            trc = P.recip(smM[:], smM[:], waits=[tsm])
            tml = P.ts(sctM, sctM, smM[:, 0:1], ALU.mult, waits=[trc])
            pu["sm"] = tml
            w1_ = [tml, t_idI] + [t for t in (pu.get("pt2"),) if t]
            ttr2 = P.transpose(B4[0:8, 0:128], sctM, idIs, waits=w1_)
            tev2 = P.act(scTM[:, 128 * tt:128 * tt + 128], B4[0:8, 0:128], AF.Copy,
                         waits=[ttr2] + [t for t in (pu.get("scT"), pu.get("scd")) if t])
            pu["pt2"] = tev2
            pu["scT"] = tev2
            pu["sct"] = ttr2
        tds = P.dma(scin.ap()[i], scTM, waits=[pu["scT"]], eng="gpsimd")
        tcs = P.copy(scoTs[:, 512 * i:512 * i + 512], scTM, waits=[pu["scT"]])
        pu["scT"] = tcs
        pu["scd"] = tds
        sc_dma.append(tds)
    P.barrier()
    ccs2 = P.sem("cc2")
    def cc2(e, s=ccs2):
        e.collective_compute(
            "AllGather", mybir.AluOpType.bypass,
            replica_groups=[[0, 2, 4, 6], [1, 3, 5, 7]],
            ins=[scin.ap().opt()], outs=[scg.ap().opt()],
        ).then_inc(s.h, 1)
    P.raw("gpsimd", cc2)
    ccs2.n += 1
    t_cc2 = (ccs2, 1)
    P.barrier()
    t_scf = []
    for r4 in range(4):
        for i2 in range(4):
            m = 4 * i2 + r4
            t_scf.append(P.dma(scfsM[8 * m:8 * m + 8, :], scg.ap()[r4, i2],
                               waits=[t_cc2]))
    tm0 = P.memset(lohi[:, 0:1], 0.0)
    tm1 = P.memset(lohi[:, 1:2], 1.0)
    tm2 = P.memset(lohi[:, 2:3], 0.5)
    pu = {}
    for it in range(NITER):
        tbc = P.mm(B3[:, 0:1], fTs[:], lohi[:, 2:3], start=True, stop=True,
                   waits=[tm2] + [t for t in (pu.get("mid"), pu.get("tbev")) if t])
        tbe = P.act(taubM, B3[:, 0:1], AF.Copy, waits=[tbc])
        pu["tbev"] = tbe
        tms = P.ts(mskM, scfsM, taubM[:, 0:1], ALU.is_ge, s2=0.0, op1=ALU.add,
                   accum_out=rowsumM[:, 0:1],
                   waits=[tbe] + t_scf + [t for t in (pu.get("cnt"),) if t])
        tcc = P.mm(B4[0:8, 0:1], fcs[:], rowsumM[:, 0:1], start=True, stop=True,
                   waits=[tms] + [t for t in (pu.get("cmp"),) if t])
        pu["cnt"] = tcc
        tcp = P.ts(cmpf[:, 0:1], B4[0:8, 0:1], CAP, ALU.is_ge,
                   waits=[tcc] + [t for t in (pu.get("sel2"),) if t])
        pu["cmp"] = tcp
        td1 = P.tt(lohi[:, 3:4], lohi[:, 2:3], lohi[:, 0:1], ALU.subtract,
                   waits=[tcp, tm0, tm1])
        ts1 = P.stt(lohi[:, 0:1], lohi[:, 3:4], cmpf[:, 0:1], lohi[:, 0:1],
                    ALU.mult, ALU.add, waits=[td1])
        td2 = P.tt(lohi[:, 3:4], lohi[:, 1:2], lohi[:, 2:3], ALU.subtract, waits=[ts1])
        ts2 = P.stt(lohi[:, 1:2], lohi[:, 3:4], cmpf[:, 0:1], lohi[:, 2:3],
                    ALU.mult, ALU.add, waits=[td2])
        pu["sel2"] = ts2
        ta = P.tt(lohi[:, 3:4], lohi[:, 0:1], lohi[:, 1:2], ALU.add, waits=[ts1, ts2])
        pu["mid"] = P.ts(lohi[:, 2:3], lohi[:, 3:4], 0.5, ALU.mult, waits=[ta])
    tau_done = pu["mid"]
    gt = {}
    for i in range(4):
        t_g1 = P.ts(gmT[:, 512 * i:512 * i + 512], scoTs[:, 512 * i:512 * i + 512],
                    lohi[:, 0:1], ALU.is_ge, waits=[tau_done])
        t_g2 = P.tt(gmT[:, 512 * i:512 * i + 512], gmT[:, 512 * i:512 * i + 512],
                    scoTs[:, 512 * i:512 * i + 512], ALU.mult, waits=[t_g1])
        for tt in range(4):
            w0 = [t_g2, t_id8z] if False else [t_g2]
            ttr = P.transpose(B5[0:128, 0:8],
                              gmT[:, 512 * i + 128 * tt:512 * i + 128 * tt + 128],
                              id8s[:], waits=w0 + [t for t in (gt.get("pt"),) if t])
            tevt = P.act(g_tm[:, 32 * i + 8 * tt:32 * i + 8 * tt + 8], B5[0:128, 0:8],
                         AF.Copy, waits=[ttr] + [t for t in (gt.get("g"),) if t])
            gt["pt"] = tevt
            gt["g"] = tevt
    t_gtm = gt["g"]
    P.barrier()

    NCAP = 768
    xg_tm = at_("xg_tm", 0, [128, 1024], BF16)
    xg_f = at_("xg_f", 2048, [128, NCIN * NCAP], BF16)
    htG = at_("htG", 14336, [128, NHC * NCAP], BF16)
    ygsb = at_("ygsb", 63488, [128, 512], F32)
    a_tm = at_("a_tm", 65536, [128, 1024], F32)
    m16f = at_("m16f", 69632, [128, 16], F32)
    m16b = at_("m16b", 69696, [128, 16], BF16)
    pfs = at_("pfs", 69760, [128, 16], F32)
    tots = at_("tots", 69824, [16, 1], F32)
    offs = at_("offs", 69856, [16, 4], F32)
    offrow = at_("offrow", 69920, [1, 16], F32)
    dsb = at_("dsb", 69984, [128, 16], F32)
    ddi = at_("ddi", 70048, [128, 16], mybir.dt.int32)
    ixsb = at_("ixsb", 70112, [128, 6], mybir.dt.int32)
    ggsb = at_("ggsb", 70144, [128, 6], F32)
    tiosb = at_("tiosb", 70176, [128, 16], mybir.dt.int32)
    c2ksb = at_("c2ksb", 70240, [1, 768], mybir.dt.int32)
    trlsb = at_("trlsb", 73312, [128, 128], BF16)
    ts16s = at_("ts16s", 73568, [16, 16], F32)
    idIbs = at_("idIbs", 73632, [128, 128], BF16)
    zrow = at_("zrow", 73888, [8, 1024], F32)

    t_tio = P.dma(tiosb, tio_d)
    t_c2k = P.dma(c2ksb, c2k_d)
    t_trl = P.dma(trlsb, trl_d)
    t_t16 = P.dma(ts16s, t16_d)
    t_idb = P.dma(idIbs, idb_d)
    t_zr = P.memset(zrow, 0.0)
    t_zrb = P.memset(zrow.bitcast(BF16)[0:1, 0:1024], 0.0, waits=[t_zr])
    t_z1 = P.dma(x3rb_d.ap()[2048:2049, :], zrow.bitcast(BF16)[0:1, 0:1024],
                 waits=[t_zrb], eng="gpsimd")
    t_g0 = P.dma(grow_d.ap()[:, 0:2048], gmT[:], waits=[t_gtm], eng="gpsimd")
    _s1 = P.dsem["gpsimd"]
    _w1 = [(t_zr[0], t_zr[1]), (_s1, _s1.n)]
    _s1.n += 16
    def _zg(e, wl=list(_w1), s=_s1):
        for ws, wv in wl:
            e.wait_ge(ws.h, wv)
        e.dma_start(out=grow_d.ap()[:, 2048:2049], in_=zrow[:, 0:1],
                    allow_slow_non_contiguous=True).then_inc(s.h, 16)
    P.ops["gpsimd"].append(_zg)
    t_g1_ = (_s1, _s1.n)

    pu4 = {}
    for tt in range(16):
        slot, tq = tt // 4, tt % 4
        tl = pool_x.dma(a_tm, x3r_d.ap()[128 * tt:128 * tt + 128, :],
                        waits=[t for t in (pu4.get("a"),) if t])
        tbs = []
        for dmh in range(2):
            tmb = P.mm(B3[:, 0:512],
                       gmT[:, 512 * slot + 128 * tq:512 * slot + 128 * tq + 128],
                       b2sM[:, 512 * dmh:512 * dmh + 512], start=True, stop=True,
                       waits=[t_gtm, t_b2l, tau_done] + [t for t in (pu4.get("b2"),) if t])
            tadd = P.tt(a_tm[:, 512 * dmh:512 * dmh + 512],
                        a_tm[:, 512 * dmh:512 * dmh + 512], B3[:, 0:512],
                        ALU.add, waits=[tmb, tl])
            pu4["b2"] = tadd
            tbs.append(tadd)
        tda = None
        for dmh in range(2):
            tda = P.dma(x4a_d[dmh].ap()[128 * tt:128 * tt + 128, :],
                        a_tm[:, 512 * dmh:512 * dmh + 512], waits=tbs, eng="gpsimd")
        pu4["a"] = tda
    P.barrier()

    scat_prev = None
    pe4 = {}
    for e in range(NE):
        gv = g_tm.rearrange("p (s t e) -> p e s t", s=4, t=4, e=8)[:, e]
        w0 = [t_gtm] + [t for t in (pe4.get("m16"),) if t]
        t1 = P.ts(m16f, gv, 0.0, ALU.is_gt, waits=w0)
        t2 = P.copy(m16b, m16f, waits=[t1])
        t3 = P.mm(B0[:, 0:16], trlsb, m16b, start=True, stop=True,
                  waits=[t2, t_trl] + [t for t in (pe4.get("B0"),) if t])
        t4 = P.act(pfs, B0[:, 0:16], AF.Copy, waits=[t3] + [t for t in (pe4.get("pfs"),) if t])
        pe4["B0"] = t4
        t5 = P.mm(B1[0:16, 0:1], m16b, obs[:], start=True, stop=True,
                  waits=[t2] + [t for t in (pe4.get("B1"),) if t])
        t6 = P.act(tots, B1[0:16, 0:1], AF.Copy, waits=[t5] + [t for t in (pe4.get("tots"),) if t])
        pe4["B1"] = t6
        t7 = P.mm(B2[0:16, 0:1], ts16s, tots, start=True, stop=True,
                  waits=[t6, t_t16] + [t for t in (pe4.get("B2"),) if t])
        t8 = P.act(offs[:, 0:1], B2[0:16, 0:1], AF.Copy, waits=[t7] + [t for t in (pe4.get("offs"),) if t])
        pe4["B2"] = t8
        t9 = P.transpose(B2[0:1, 0:16], offs[:, 0:1], idIs[0:16, 0:16], waits=[t8])
        t10 = P.act(offrow, B2[0:1, 0:16], AF.Copy, waits=[t9] + [t for t in (pe4.get("orow"),) if t])
        pe4["offs"] = t10
        t11 = P.mm(B2[:, 0:16], o1s[:], offrow, start=True, stop=True, waits=[t10])
        t12 = P.ts(dsb, pfs, -1.0, ALU.add, waits=[t4] + [t for t in (pe4.get("dsb"),) if t])
        pe4["pfs"] = t12
        t13 = P.tt(dsb, dsb, B2[:, 0:16], ALU.add, waits=[t12, t11])
        pe4["B2"] = t13
        pe4["orow"] = t13
        t14 = P.ts(dsb, dsb, -2048.0, ALU.add, waits=[t13])
        t15 = P.tt(dsb, dsb, m16f, ALU.mult, waits=[t14])
        pe4["m16"] = t15
        t16 = P.ts(dsb, dsb, 2048.0, ALU.add, waits=[t15])
        t17 = P.copy(ddi, dsb, waits=[t16] + list(pe4.get("ddi") or ()))
        pe4["dsb"] = t17
        wpre = [t_c2k] + [t for t in (pe4.get("ixu"),) if t]
        tpre = P.dma(ix_d.ap()[e, 0:768, 0], c2ksb[0:1, :], waits=wpre, eng="gpsimd")
        sct = []
        for tile in range(16):
            wl = [tpre, t17, t_tio]
            sct.append(P.idma(pool_ix, (lambda e_, e=e, tile=tile: e_.indirect_dma_start(
                ix_d.ap().rearrange("e t o -> (e t) o"),
                IndirectOffsetOnAxis(ddi[:, tile:tile + 1], 0),
                tiosb[:, tile:tile + 1], None, element_offset=2049 * e)), waits=wl))
        pe4["ddi"] = sct[-4:]
        t_ix = []
        for g in range(6):
            t_ix.append(P.dma(ixsb[:, g:g + 1], ix_d.ap()[e, 128 * g:128 * g + 128, 0:1],
                              waits=sct + [t for t in (pe4.get("ixc"), scat_prev) if t]))
        pe4["ixu"] = t_ix[-1]
        tgg = []
        txg = []
        for g in range(6):
            wl = [t_ix[g], t_g0, t_g1_, t_z1] + [t for t in (pe4.get("gg"), pe4.get("xgf", )) if t]
            tgg.append(P.idma(pool_gx, (lambda e_, e=e, g=g: e_.indirect_dma_start(
                ggsb[:, g:g + 1], None,
                grow_d.ap().rearrange("e t -> (e t)").unsqueeze(1),
                IndirectOffsetOnAxis(ixsb[:, g:g + 1], 0),
                element_offset=2049 * e)), waits=wl))
            wl2 = [t_ix[g]] + [t for t in (pe4.get(("xgu", 0)),) if t]
            t_xgm = P.idma(pool_gx, (lambda e_, g=g: e_.indirect_dma_start(
                xg_tm, None, x3rb_d.ap(), IndirectOffsetOnAxis(ixsb[:, g:g + 1], 0))),
                waits=wl2)
            for ci in range(NCIN):
                wtr = [t_xgm, t_idb] + [t for t in (pe4.get("PT"),) if t]
                ttr = P.transpose(PT[:, 0:128], xg_tm[:, 128 * ci:128 * ci + 128],
                                  idIbs, waits=wtr)
                tevt = P.copy(xg_f[:, NCAP * ci + 128 * g:NCAP * ci + 128 * g + 128],
                              PT[:, 0:128],
                              waits=[ttr] + [t for t in (pe4.get("xgf2"),) if t])
                pe4["PT"] = tevt
                pe4["xgf2"] = tevt
                txg.append(tevt)
            pe4[("xgu", 0)] = pe4["PT"]
        pe4["gg"] = tgg[-1]
        pe4["ixc"] = pe4["PT"]
        pe4["xgf"] = pe4["PT"]
        ev_last = None
        slabsG = [(0, 512), (512, 256)]
        for hc in range(NHC):
            k = hc % 4
            wst_ = [pe4[("st", k)]] if ("st", k) in pe4 else []
            tl = pool_w.dma(r3(stgM[k], NCIN, 128),
                            w1_d[e][:, 128 * hc:128 * hc + 128]
                            .rearrange("(c p) h -> p c h", p=128), waits=wst_)
            tc = P.copy(w1bM[k][:], stgM[k],
                        waits=[tl] + ([pe4[("mmk", k)]] if ("mmk", k) in pe4 else []))
            pe4[("st", k)] = tc
            for hf, (s0, sw) in enumerate(slabsG):
                bank = pp[hf]
                w0 = [tc] + ([pe4[("hp", hf)]] if ("hp", hf) in pe4 else [])
                pairs = [(w1bM[k][:, 128 * ci:128 * ci + 128],
                          xg_f[:, NCAP * ci + s0:NCAP * ci + s0 + sw])
                         for ci in range(NCIN)]
                tmm = P.mmchain(bank[:, 0:sw], pairs, waits=w0 + txg[-1:])
                pe4[("mmk", k)] = tmm
                wev = [tmm, t_b1l]
                if hc == 0 and hf == 0 and ("ht",) in pe4:
                    wev.append(pe4[("ht",)])
                ev_last = P.act(htG[:, NCAP * hc + s0:NCAP * hc + s0 + sw],
                                bank[:, 0:sw], AF.Gelu,
                                bias=b1sM[:, NHC * e + hc:NHC * e + hc + 1],
                                waits=wev)
                pe4[("hp", hf)] = ev_last
        pyb = [B0, B1, B2, B3, B4, B5]
        for dmh in range(2):
            ymm = {}
            for hc in range(NHC):
                k = hc % 4
                wst_ = [pe4[("st", k)]] if ("st", k) in pe4 else []
                tl = pool_w.dma(stgM[k][:, 0:512],
                                w2_d[e][128 * hc:128 * hc + 128,
                                        512 * dmh:512 * dmh + 512], waits=wst_)
                tc = P.copy(w2bM[k][:], stgM[k][:, 0:512],
                            waits=[tl] + ([pe4[("ymk", k)]] if ("ymk", k) in pe4 else []))
                pe4[("st", k)] = tc
                for g in range(6):
                    w0 = [tc, ev_last]
                    if hc == 0 and ("py", g) in pe4:
                        w0.append(pe4[("py", g)])
                    ymm[g] = P.mm(pyb[g][:], htG[:, NCAP * hc + 128 * g:NCAP * hc + 128 * g + 128],
                                  w2bM[k][:], start=(hc == 0), stop=(hc == NHC - 1),
                                  waits=w0)
                    pe4[("ymk", k)] = ymm[g]
            for g in range(6):
                yslot = g % 2
                wyg = [ymm[g], tgg[g]] + [t for t in (pe4.get(("ygu", yslot)),) if t]
                tyg = P.ts(ygs2[yslot], pyb[g][:], ggsb[:, g:g + 1], ALU.mult, waits=wyg)
                pe4[("py", g)] = tyg
                wl3 = [tyg] + ([scat_prev] if scat_prev else [])
                scat_prev = P.idma(pool_sc, (lambda e_, g=g, dmh=dmh, yslot=yslot: e_.indirect_dma_start(
                    x4a_d[dmh].ap(),
                    IndirectOffsetOnAxis(ixsb[:, g:g + 1], 0),
                    ygs2[yslot], None, compute_op=ALU.add)), waits=wl3)
                pe4[("ygu", yslot)] = scat_prev
        pe4[("ht",)] = P.esem["tensor"].tag()
    P.barrier()
    P.wait("gpsimd", [scat_prev])

    pl = {}
    for tt in range(16):
        slot, tq = tt // 4, tt % 4
        tla = []
        for dmh in range(2):
            tla.append(pool_x.dma(a_tm[:, 512 * dmh:512 * dmh + 512],
                                  x4a_d[dmh].ap()[128 * tt:128 * tt + 128, :],
                                  waits=[scat_prev] + [t for t in (pl.get("a"),) if t]))
        av = a_tm
        w0 = list(tla) + ([pl["sq"]] if "sq" in pl else [])
        tsq = P.act(sqtM, av, AF.Square, waits=w0)
        tm_ = P.emit("vector", lambda e_, av=av: e_.tensor_reduce(
            out=mrow4[:, 0:1], in_=av, axis=mybir.AxisListType.X, op=ALU.add),
            waits=list(tla) + ([pl["m"]] if "m" in pl else []))
        ts2_ = P.emit("vector", lambda e_: e_.tensor_reduce(
            out=mrow4[:, 1:2], in_=sqtM, axis=mybir.AxisListType.X, op=ALU.add),
            waits=[tsq] + ([pl["s2"]] if "s2" in pl else []))
        pl["sq"] = ts2_
        t1 = P.ts(mrow4[:, 0:1], mrow4[:, 0:1], 1.0 / D, ALU.mult, waits=[tm_])
        t2 = P.ts(mrow4[:, 1:2], mrow4[:, 1:2], 1.0 / D, ALU.mult, waits=[ts2_])
        t3 = P.tt(mrow4[:, 2:3], mrow4[:, 0:1], mrow4[:, 0:1], ALU.mult,
                  waits=[t1] + ([pl["v"]] if "v" in pl else []))
        t4 = P.tt(mrow4[:, 2:3], mrow4[:, 1:2], mrow4[:, 2:3], ALU.subtract,
                  waits=[t2, t3])
        t5 = P.ts(mrow4[:, 2:3], mrow4[:, 2:3], 1e-5, ALU.add, waits=[t4])
        t6 = P.act(mrow4[:, 2:3], mrow4[:, 2:3], AF.Sqrt, waits=[t5])
        t7 = P.recip(mrow4[:, 3:4], mrow4[:, 2:3],
                     waits=[t6] + ([pl["r"]] if "r" in pl else []))
        t8 = P.ts(x4tM, av, mrow4[:, 0:1], ALU.subtract, s2=mrow4[:, 3:4],
                  op1=ALU.mult, waits=[t7, t1] + ([pl["x4"]] if "x4" in pl else []))
        pl["m"] = t8
        pl["s2"] = t8
        pl["v"] = t8
        pl["r"] = t8
        pl["a"] = t8
        td = P.dma(out_d[slot, 128 * tq:128 * tq + 128, :], x4tM,
                   waits=[t8], eng="gpsimd")
        pl["x4"] = td
    P.barrier()

    P.wait("gpsimd", P.all_tags())
    P.build()
    st.close()
    return nc


def kernel(**inputs):
    nc = build_fused(dbg=False)
    im = host_prep_fused(inputs)
    res = run_bass_kernel_spmd(nc, im, core_ids=list(range(8)))
    out = np.zeros((2, 8192, 1024), np.float32)
    for c in range(8):
        for i in range(4):
            seq, m = slab_of(c, i)
            out[seq, 512 * m:512 * m + 512] = np.array(res.results[c]["out"])[i]
    return out


# version 09 (final)
# speedup vs baseline: 1.1162x; 1.1162x over previous
import numpy as np
from contextlib import ExitStack
import ml_dtypes
import concourse.bass as bass
import concourse.mybir as mybir
from concourse.bass_core_rust import IndirectOffsetOnAxis
from concourse.bass_utils import run_bass_kernel_spmd


F32 = mybir.dt.float32
BF16 = mybir.dt.bfloat16
U8 = mybir.dt.uint8
AF = mybir.ActivationFunctionType
ALU = mybir.AluOpType
AX = mybir.AxisListType

ENGINES = ("sync", "tensor", "vector", "scalar", "gpsimd")

class Sem:
    __slots__ = ("h", "n", "name")
    def __init__(self, h, name):
        self.h = h; self.n = 0; self.name = name
    def tag(self):
        return (self, self.n)

def _dedup(waits):
    best = {}
    for s, v in waits:
        if s is not None and (s not in best or v > best[s]):
            best[s] = v
    return [(s, v) for s, v in best.items()]

class Prog:
    def __init__(self, nc, stack):
        self.nc = nc
        self.stack = stack
        self.ops = {e: [] for e in ENGINES}
        self.hwm = {e: {} for e in ENGINES}
        self._semcnt = 0
        self.esem = {e: self.sem(f"es_{e}") for e in ENGINES}
        self.dsem = {e: self.sem(f"ds_{e}") for e in ("sync", "gpsimd", "scalar", "vector")}

    def sem(self, name=None):
        if name is None:
            name = f"s{self._semcnt}"
        self._semcnt += 1
        h = self.stack.enter_context(self.nc.semaphore(name))
        return Sem(h, name)

    def filt(self, eng, wl):
        hw = self.hwm[eng]
        out = []
        for s2, v in _dedup(wl):
            if v > hw.get(s2, 0):
                hw[s2] = v
                out.append((s2, v))
        return out

    def emit(self, eng, fn, waits=(), dma=False, inc=1, noinc=False):
        s = self.dsem[eng] if dma else self.esem[eng]
        k = 16 if dma else inc
        wl = list(waits)
        if dma:
            wl.append((s, s.n))
        if not noinc:
            s.n += k
        wl = self.filt(eng, wl)
        def op(e, fn=fn, wl=wl, s=s, k=k, noinc=noinc):
            for ws, wv in wl:
                e.wait_ge(ws.h, wv)
            inst = fn(e)
            if not noinc:
                inst.then_inc(s.h, k)
        self.ops[eng].append(op)
        return (s, s.n)

    def dma(self, out, in_, waits=(), eng="sync"):
        return self.emit(eng, lambda e: e.dma_start(out=out, in_=in_), waits, dma=True)

    def mm(self, out, lhsT, rhs, start, stop, waits=()):
        return self.emit("tensor", lambda e: e.matmul(out, lhsT, rhs, start=start, stop=stop), waits)

    def mmchain(self, out, pairs, waits=()):
        n = len(pairs)
        t = None
        for i, (l, r) in enumerate(pairs):
            t = self.emit("tensor", (lambda e, l=l, r=r, i=i, n=n, out=out:
                          e.matmul(out, l, r, start=(i == 0), stop=(i == n - 1))),
                          waits=(waits if i == 0 else ()), noinc=(i != n - 1))
        return t

    def act(self, out, in_, func, bias=0.0, scale=1.0, waits=(), accum_out=None):
        return self.emit("scalar", lambda e: e.activation(
            out=out, in_=in_, func=func, bias=bias, scale=scale,
            accum_out=accum_out), waits)

    def tt(self, out, in0, in1, op, waits=(), eng="vector"):
        return self.emit(eng, lambda e: e.tensor_tensor(out=out, in0=in0, in1=in1, op=op), waits)

    def ts(self, out, in0, s1, op0, s2=None, op1=None, waits=(), accum_out=None, eng="vector"):
        if s2 is None and accum_out is None:
            return self.emit(eng, lambda e: e.tensor_single_scalar(out=out, in_=in0, scalar=s1, op=op0), waits)
        kw = dict(out=out, in0=in0, scalar1=s1, scalar2=s2, op0=op0, accum_out=accum_out)
        if op1 is not None:
            kw["op1"] = op1
        return self.emit(eng, lambda e: e.tensor_scalar(**kw), waits)

    def stt(self, out, in0, scalar, in1, op0, op1, waits=(), eng="vector"):
        return self.emit(eng, lambda e: e.scalar_tensor_tensor(
            out=out, in0=in0, scalar=scalar, in1=in1, op0=op0, op1=op1), waits)

    def copy(self, out, in_, waits=(), eng="vector"):
        return self.emit(eng, lambda e: e.tensor_copy(out=out, in_=in_), waits)

    def memset(self, ap, val, waits=(), eng="vector"):
        return self.emit(eng, lambda e: e.memset(ap, val), waits)

    def recip(self, out, in_, waits=()):
        return self.emit("vector", lambda e: e.reciprocal(out=out, in_=in_), waits)

    def red(self, out, in_, op, waits=(), negate=None):
        return self.emit("vector", lambda e: e.tensor_reduce(
            out=out, in_=in_, axis=AX.X, op=op, negate=negate), waits)

    def sel(self, out, mask, on_true, on_false, waits=()):
        return self.emit("vector", lambda e: e.select(
            out=out, mask=mask, on_true=on_true, on_false=on_false), waits)

    def transpose(self, out, in_, ident, waits=()):
        return self.emit("tensor", lambda e: e.transpose(out, in_, ident), waits)

    def wait(self, eng, waits):
        wl = self.filt(eng, waits)
        def op(e, wl=wl):
            for ws, wv in wl:
                e.wait_ge(ws.h, wv)
        self.ops[eng].append(op)

    def all_tags(self):
        return [(s, s.n) for s in list(self.esem.values()) + list(self.dsem.values())]

    def build(self):
        nc = self.nc
        with nc.Block() as block:
            @block.sync
            def _(e):
                for op in self.ops["sync"]: op(e)
            @block.tensor
            def _(e):
                for op in self.ops["tensor"]: op(e)
            @block.vector
            def _(e):
                for op in self.ops["vector"]: op(e)
            @block.scalar
            def _(e):
                for op in self.ops["scalar"]: op(e)
            @block.gpsimd
            def _(e):
                for op in self.ops["gpsimd"]: op(e)

def f32(x):
    return np.ascontiguousarray(np.asarray(x), dtype=np.float32)

def bf16(x):
    return np.ascontiguousarray(np.asarray(x).astype(ml_dtypes.bfloat16))

def r3(ap, c, w):
    return ap.rearrange("p (c w) -> p c w", c=c, w=w)

def bc3(ap, c):
    return ap.unsqueeze(1).broadcast_to([ap.shape[0], c, ap.shape[-1]])

def _add_barrier(P):
    tags = P.filt("gpsimd", P.all_tags())
    b = P.sem()
    def gather(e, tags=tags, b=b):
        for ws, wv in tags:
            e.wait_ge(ws.h, wv)
        e.sem_inc(b.h, 1)
    P.ops["gpsimd"].append(gather)
    b.n += 1
    t = (b, b.n)
    for eng in ENGINES:
        if eng != "gpsimd":
            P.wait(eng, [t])
    return t

Prog.barrier = _add_barrier

def _raw(P, eng, fn):
    P.ops[eng].append(fn)

Prog.raw = _raw

class DmaPool:
    def __init__(self, P, name, n):
        self.sems = [P.sem(f"{name}{i}") for i in range(n)]
        self.i = 0
        self.P = P

    def dma(self, out, in_, waits=(), eng="sync"):
        s = self.sems[self.i % len(self.sems)]
        self.i += 1
        wl = list(waits) + [(s, s.n)]
        s.n += 16
        wl = self.P.filt(eng, wl)
        def op(e, wl=wl, s=s, out=out, in_=in_):
            for ws, wv in wl:
                e.wait_ge(ws.h, wv)
            e.dma_start(out=out, in_=in_).then_inc(s.h, 16)
        self.P.ops[eng].append(op)
        return (s, s.n)

def _mkpool(P, name, n):
    return DmaPool(P, name, n)

Prog.pool = _mkpool

def _idma(P, pool, fn, waits=(), eng="gpsimd"):
    s = pool.sems[pool.i % len(pool.sems)]
    pool.i += 1
    wl = list(waits) + [(s, s.n)]
    s.n += 16
    wl = P.filt(eng, wl)
    def op(e, fn=fn, wl=wl, s=s):
        for ws, wv in wl:
            e.wait_ge(ws.h, wv)
        fn(e).then_inc(s.h, 16)
    P.ops[eng].append(op)
    return (s, s.n)

Prog.idma = _idma


NCIN = 8
D = 1024
NH = 8
DH = 128
NE = 8
NHC = 32
S_SEQ = 8192
WID = 768
XW = 513
INV32 = 1.0 / 32.0
INV128 = 1.0 / 128.0
PEXT = (16, 32, 48, 64)
CAP = 2049.0
NITER = 48
NEG = -1e30
AB = 53248


def slab_of(c, i):
    return c % 2, 4 * i + c // 2


def host_prep_fused(ins):
    x = f32(ins["L0_attn_sequence_output"])
    b3 = np.zeros((128, 24), np.float32)
    for ib, b in enumerate((ins["L0_attn_bq"], ins["L0_attn_bk"], ins["L0_attn_bv"])):
        b3[:, 8 * ib:8 * ib + 8] = f32(b).reshape(8, 128).T
    mu3 = np.zeros((128, 16), np.float32)
    mu3[:, 0:8] = f32(ins["L0_ffn_mu_r"]).reshape(8, 128).T
    mu3[:, 8:16] = f32(ins["L0_ffn_mu_k"]).reshape(8, 128).T
    b1p = np.zeros((128, NE * NHC), np.float32)
    for e in range(NE):
        b1p[:, NHC * e:NHC * e + NHC] = f32(ins["L1_ffn_b1"][e]).reshape(NHC, 128).T
    foldT = np.zeros((8, 128), np.float32)
    foldc = np.zeros((128, 8), np.float32)
    for p in range(128):
        foldT[p % 8, p] = 1.0
        foldc[p, p % 8] = 1.0
    cm = np.zeros((4, 128, 512), np.float32)
    for jt in range(4):
        cm[jt] = (128 * jt + np.arange(128)[:, None]) <= np.arange(512)[None, :]
    dims = np.arange(32).astype(np.float32)
    af = np.power(np.float32(10000.0), -dims / np.float32(32.0)).astype(np.float32)
    shared = dict(
        wq=f32(ins["L0_attn_Wq"]), wk=f32(ins["L0_attn_Wk"]), wv=f32(ins["L0_attn_Wv"]),
        b3=b3, bvrow=f32(ins["L0_attn_bv"]).reshape(1, D),
        wr=f32(ins["L0_ffn_W_r"]).T.copy(), wk2=f32(ins["L0_ffn_W_k"]).T.copy(),
        wv2=f32(ins["L0_ffn_W_v"]).T.copy(), mu3=mu3,
        wq1=f32(ins["L1_attn_wq"]), wk1=f32(ins["L1_attn_wk"]),
        wv1=f32(ins["L1_attn_wv"]), wo=f32(ins["L1_attn_wo"]),
        rwT=f32(ins["L1_ffn_router_w"]).T.copy(),
        w1=f32(ins["L1_ffn_w1"]), w2=f32(ins["L1_ffn_w2"]), b1p=b1p,
        b2e=f32(ins["L1_ffn_b2"]),
        ones_bf=bf16(np.ones((128, 1))), ones_f1=f32(np.ones((1, 128))),
        id8=f32(np.eye(8)), idI=f32(np.eye(128)), foldT=foldT, foldc=foldc, cm=bf16(cm),
        trilb=bf16(np.tril(np.ones((128, 128), np.float32)).T),
        ts16=f32(np.triu(np.ones((16, 16), np.float32), 1)),
        tio=np.ascontiguousarray((np.arange(16)[None, :] * 128
                                  + np.arange(128)[:, None]).astype(np.int32)),
        c2048=np.full((1, 768), 2048, np.int32),
        idIb=bf16(np.eye(128)),
    )
    in_maps = []
    for c in range(8):
        xa = np.zeros((4, D, WID), np.float32)
        masks = np.zeros((4, 5, 128, 384), np.float32)
        cs = np.zeros((4, 32, 512), np.float32)
        sn = np.zeros((4, 32, 512), np.float32)
        pmulv = np.zeros((4, 128, 1), np.float32)
        tmask = np.zeros((4, 128, 64), np.float32)
        for i in range(4):
            seq, m = slab_of(c, i)
            start = 512 * m
            lo, hi = start - 128, start + 640
            slo, shi = max(lo, 0), min(hi, S_SEQ)
            xa[i][:, slo - lo: shi - lo] = x[seq, slo:shi, :].T
            for j in range(-1, 4):
                qa = start + 128 * j
                qg = qa + np.arange(128)[None, :]
                qv = (qg >= 0) & (qg < S_SEQ)
                for p in range(3):
                    kg = qa + 128 * (p - 1) + np.arange(128)[:, None]
                    kv = (kg >= 0) & (kg < S_SEQ)
                    mm_ = (np.abs(kg - qg) <= 2) & kv & qv
                    mm_ = np.where(qv, mm_, kv)
                    if j == -1 and p == 0:
                        mm_ = np.zeros_like(mm_)
                    masks[i, j + 1, :, 128 * p:128 * p + 128] = mm_
            pos = (start + np.arange(512)).astype(np.float32)
            rad = (pos[None, :] * af[:, None]).astype(np.float32)
            cs[i] = np.cos(rad.astype(np.float64)).astype(np.float32)
            sn[i] = np.sin(rad.astype(np.float64)).astype(np.float32)
            pmulv[i, :, 0] = 0.0 if m == 0 else 1.0
            E = 4 * m + 4
            tmask[i, :, :] = np.where(np.arange(64)[None, :] < E - 4, 0.0, NEG)
        im = dict(shared)
        im.update(xa=f32(xa), masks=bf16(masks), cs=cs, sn=sn, pmulv=pmulv,
                  tmask=tmask)
        in_maps.append(im)
    return in_maps


def _ln_fm(P, T, xsrc, wid_in, win, dst, pvx):
    obs, o1s, pz, pz2, pp, srow = T["obs"], T["o1s"], T["pz"], T["pz2"], T["pp"], T["srow"]
    rb2, sq2 = T["lnrb"], T["lnsq"]
    for (s0, sw, oc) in win:
        xv = r3(xsrc, NCIN, wid_in)[:, :, s0:s0 + sw]
        rbv = r3(rb2, 8, 512)[:, :, 0:sw]
        sqv = r3(sq2, 8, 512)[:, :, 0:sw]
        tc = P.copy(rbv, xv, waits=[t for t in (pvx.get("m1"),) if t])
        tsq = P.act(sqv, rbv, AF.Square, waits=[tc] + [t for t in (pvx.get("m2"),) if t])
        m1 = m2 = None
        for ci in range(NCIN):
            w0 = [tc] + [t for t in (pvx.get("tm_"),) if t] if ci == 0 else []
            m1 = P.mm(pz[0:1, 0:sw], obs[:], rb2[:, 512 * ci:512 * ci + sw],
                      start=(ci == 0), stop=(ci == NCIN - 1), waits=w0)
        for ci in range(NCIN):
            w0 = [tsq] + [t for t in (pvx.get("t8"),) if t] if ci == 0 else []
            m2 = P.mm(pz2[0:1, 0:sw], obs[:], sq2[:, 512 * ci:512 * ci + sw],
                      start=(ci == 0), stop=(ci == NCIN - 1), waits=w0)
        mslot_ = srow[0:1, 0:sw]
        msq = srow[0:1, 512:512 + sw]
        vars_ = srow[0:1, 1024:1024 + sw]
        rs = srow[0:1, 1536:1536 + sw]
        tm_ = P.ts(mslot_, pz[0:1, 0:sw], 1.0 / D, ALU.mult,
                   waits=[m1] + [t for t in (pvx.get("b1"), pvx.get("t7")) if t])
        t7 = P.tt(msq, mslot_, mslot_, ALU.mult,
                  waits=[tm_] + [t for t in (pvx.get("t8"),) if t])
        t8 = P.stt(vars_, pz2[0:1, 0:sw], 1.0 / D, msq, ALU.mult, ALU.subtract,
                   waits=[t7, m2] + [t for t in (pvx.get("t10"),) if t])
        t8b = P.ts(vars_, vars_, 1e-5, ALU.add, waits=[t8])
        t9 = P.act(vars_, vars_, AF.Sqrt, waits=[t8b])
        t10 = P.recip(rs, vars_, waits=[t9] + [t for t in (pvx.get("b2"),) if t])
        b1_ = P.mm(pp[0][:, 0:sw], o1s[:], mslot_, start=True, stop=True,
                   waits=[tm_] + [t for t in (pvx.get("t11"),) if t])
        b2_ = P.mm(pp[1][:, 0:sw], o1s[:], rs, start=True, stop=True,
                   waits=[t10] + [t for t in (pvx.get("t12"),) if t])
        tv_ = r3(T["lnout"], 8, 512)[:, :, 0:sw]
        t11 = P.tt(tv_, xv, bc3(pp[0][:, 0:sw], 8), ALU.subtract,
                   waits=[b1_] + [t for t in (pvx.get("dma"),) if t])
        t12 = P.tt(tv_, tv_, bc3(pp[1][:, 0:sw], 8), ALU.mult, waits=[t11, b2_])
        tdma = dst(oc, sw, t12)
        pvx.clear()
        pvx.update(m1=m1, m2=m2, tm_=tm_, t7=t7, t8=t8, t10=t10, b1=b1_, b2=b2_,
                   t11=t11, t12=t12, dma=tdma)
    return pvx


def build_fused(dbg=False):
    nc = bass.Bass("TRN2", target_bir_lowering=False)
    st = ExitStack()
    P = Prog(nc, st)
    din = lambda n, s, d: nc.dram_tensor(n, s, d, kind="ExternalInput").ap()

    xa = din("xa", [4, D, WID], F32)
    wq_d = din("wq", [D, D], F32)
    wk_d = din("wk", [D, D], F32)
    wv_d = din("wv", [D, D], F32)
    b3_d = din("b3", [128, 24], F32)
    bvr_d = din("bvrow", [1, D], F32)
    mk_d = din("masks", [4, 5, 128, 384], BF16)
    ob_d = din("ones_bf", [128, 1], BF16)
    o1_d = din("ones_f1", [1, 128], F32)
    wr_d = din("wr", [D, D], F32)
    wk2_d = din("wk2", [D, D], F32)
    wv2_d = din("wv2", [D, D], F32)
    mu_d = din("mu3", [128, 16], F32)
    pm_d = din("pmulv", [4, 128, 1], F32)
    wq1_d = din("wq1", [NH, D, DH], F32)
    wk1_d = din("wk1", [NH, D, DH], F32)
    wv1_d = din("wv1", [NH, D, DH], F32)
    cs_d = din("cs", [4, 32, 512], F32)
    sn_d = din("sn", [4, 32, 512], F32)
    wo_d = din("wo", [D, D], F32)
    cm_d = din("cm", [4, 128, 512], BF16)
    tm_d = din("tmask", [4, 128, 64], F32)
    rw_d = din("rwT", [D, 8], F32)
    id8_d = din("id8", [8, 8], F32)
    idI_d = din("idI", [128, 128], F32)
    fT_d = din("foldT", [8, 128], F32)
    fc_d = din("foldc", [128, 8], F32)
    w1_d = din("w1", [NE, D, 4096], F32)
    w2_d = din("w2", [NE, 4096, D], F32)
    b1_d = din("b1p", [128, NE * NHC], F32)
    b2_d = din("b2e", [8, D], F32)
    trl_d = din("trilb", [128, 128], BF16)
    t16_d = din("ts16", [16, 16], F32)
    tio_d = din("tio", [128, 16], mybir.dt.int32)
    c2k_d = din("c2048", [1, 768], mybir.dt.int32)
    idb_d = din("idIb", [128, 128], BF16)
    out_d = nc.dram_tensor("out", [4, 512, D], F32, kind="ExternalOutput").ap()

    ikind = "ExternalOutput" if dbg else "Internal"
    x1_d = nc.dram_tensor("x1d", [4, D, XW], F32, kind=ikind).ap()
    x2_d = nc.dram_tensor("x2d", [4, D, 512], F32, kind=ikind).ap()
    x3_d = nc.dram_tensor("x3d", [4, D, 512], F32, kind=ikind).ap()
    x3r_d = nc.dram_tensor("x3rd", [2049, D], F32)
    x3rb_d = nc.dram_tensor("x3rbd", [2049, D], BF16)
    grow_d = nc.dram_tensor("growd", [8, 2049], F32)
    ix_d = nc.dram_tensor("ixd", [8, 2049, 1], mybir.dt.int32)
    x4a_d = [nc.dram_tensor("x4ad0", [2049, 512], F32),
             nc.dram_tensor("x4ad1", [2049, 512], F32)]
    qo_d = nc.dram_tensor("qod", [4, NH, DH, 512], BF16).ap()
    kvin = nc.dram_tensor("kvin", [2, NH, 4, 128, 512], BF16)
    kvg = nc.dram_tensor("kvg", [4, 2, NH, 4, 128, 512], BF16)
    nrin = nc.dram_tensor("nrin", [1, 32], F32)
    nrg = nc.dram_tensor("nrg", [4, 32], F32)
    scin = nc.dram_tensor("scin", [4, 8, 512], F32)
    scg = nc.dram_tensor("scg", [4, 4, 8, 512], F32)

    pool_w = P.pool("pw", 8)
    pool_x = P.pool("px", 4)
    pool_kv = P.pool("pkv", 8)
    pool_ix = P.pool("pix", 4)
    pool_gx = P.pool("pgx", 4)
    pool_sc = P.pool("psc", 1)
    sbuf = lambda n, s, d: st.enter_context(nc.sbuf_tensor(n, s, d))
    psum = lambda n, s, d: st.enter_context(nc.psum_tensor(n, s, d))
    b3s = sbuf("b3s", [128, 24], F32)
    bvr = sbuf("bvr", [128, D], F32)
    obs = sbuf("obs", [128, 1], BF16)
    o1s = sbuf("o1s", [1, 128], F32)
    mus = sbuf("mus", [128, 16], F32)
    id8s = sbuf("id8s", [8, 8], F32)
    fTs = sbuf("fTs", [8, 128], F32)
    fcs = sbuf("fcs", [128, 8], F32)
    cms = sbuf("cms", [128, 4 * 512], BF16)
    mrow = sbuf("mrow", [1, 64], F32)
    srow = sbuf("srow", [1, 4 * 512], F32)
    biasM = sbuf("biasM", [128, 1], F32)
    scoTs = sbuf("scoTs", [8, 4 * 512], F32)
    gmT = sbuf("gmT", [8, 4 * 512], F32)
    g_tm = sbuf("g_tm", [128, 128], F32)
    lohi = sbuf("lohi", [8, 4], F32)
    cmpf = sbuf("cmpf", [8, 2], F32)
    nr4 = sbuf("nr4", [1, 128], F32)
    maxk2 = sbuf("maxk2", [1, 8], F32)
    idIsp = sbuf("idIsp", [128, 128], F32)

    B0 = psum("B0", [128, 512], F32)
    B1 = psum("B1", [128, 512], F32)
    B2 = psum("B2", [128, 512], F32)
    B3 = psum("B3", [128, 512], F32)
    B4 = psum("B4", [128, 512], F32)
    B5 = psum("B5", [128, 512], F32)
    R0 = psum("R0", [1, 512], F32)
    PT = psum("PT", [128, 256], BF16)
    R1 = None
    pp = [B0, B1]
    pz = R0
    pz2 = B5[0:1, 0:512]
    pzb = B2

    def at_(name, off, shape, dt):
        h = nc.alloc_sbuf_tensor_at(name, shape, dt, offset=AB + off)
        return h[:]

    P.dma(obs[:], ob_d)
    P.dma(o1s[:], o1_d)
    P.dma(b3s[:], b3_d)
    P.dma(mus[:], mu_d)
    P.dma(id8s[:], id8_d)
    P.dma(fTs[:], fT_d)
    P.dma(fcs[:], fc_d)
    for j in range(4):
        P.dma(cms[:, 512 * j:512 * j + 512], cm_d[j])

    T = dict(obs=obs, o1s=o1s, pz=pz, pz2=pz2, pp=pp, srow=srow)

    xt = at_("xtA", 0, [128, NCIN * WID], F32)
    xtb = at_("xtbA", 24576, [128, NCIN * WID], BF16)
    wst = at_("wstA", 36864, [128, 1024], F32)
    wA = at_("wA", 40960, [128, 3 * NCIN * 1024], BF16)
    qt = at_("qtA", 90112, [128, NCIN * WID], BF16)
    kt = at_("ktA", 102400, [128, NCIN * WID], BF16)
    vt = at_("vtA", 114688, [128, 6 * 1024], BF16)
    mka = at_("mkaA", 126976, [128, 2 * 384], BF16)
    expm = at_("expmA", 128512, [128, 384], BF16)
    zrA = at_("zrA", 129280, [1, 128], F32)
    o_sb = at_("osbA", 129792, [128, 1024], F32)
    t_sbA = at_("tsbA", 133888, [128, 4096], F32)
    sqm = at_("sqm", 166688, [128, 1024], BF16)
    wst2 = at_("wstA2", 168736, [128, 1024], F32)
    lnA1 = at_("lnA1", 90112, [128, 4096], BF16)
    lnA2 = at_("lnA2", 98304, [128, 4096], BF16)
    x1t = at_("x1tA", 150272, [128, NCIN * XW], F32)
    T["lnrb"] = lnA1
    T["lnsq"] = lnA2
    T["lnout"] = t_sbA

    wAq = lambda wi, ci: wA[:, 1024 * (NCIN * wi + ci):1024 * (NCIN * wi + ci) + 1024]
    stgA_last = [None, None]
    cntA = 0
    for wi, wd in enumerate((wq_d, wk_d, wv_d)):
        for ci in range(NCIN):
            par = cntA % 2
            cntA += 1
            stg = (wst, wst2)[par]
            tl = pool_w.dma(stg, wd[128 * ci:128 * ci + 128, :],
                            waits=([stgA_last[par]] if stgA_last[par] else []))
            stgA_last[par] = P.copy(wAq(wi, ci), stg, waits=[tl])
    t_bvl = P.dma(wst[0:1, :], bvr_d, waits=[t for t in stgA_last if t])
    for h in range(2):
        tmz = P.mm(pp[h][:, 0:512], o1s[:], wst[0:1, 512 * h:512 * h + 512],
                   start=True, stop=True, waits=[t_bvl])
        P.act(bvr[:, 512 * h:512 * h + 512], pp[h][:, 0:512], AF.Copy, waits=[tmz])
    P.barrier()

    slabs768 = [(0, 512), (512, 256)]
    for i in range(4):
        t_xc = []
        for ci in range(NCIN):
            tl = pool_x.dma(xt[:, WID * ci:WID * ci + WID],
                            xa[i, 128 * ci:128 * ci + 128, :])
            t_xc.append(P.copy(xtb[:, WID * ci:WID * ci + WID],
                               xt[:, WID * ci:WID * ci + WID], waits=[tl]))
        ppi = 0
        ppb = [B0, B1, B2, B3]
        pp_free = [None, None, None, None]
        for wi, outb in ((0, qt), (1, kt)):
            for do in range(NCIN):
                for (s0, sw) in slabs768:
                    bank = ppb[ppi % 4]
                    waits = list(t_xc)
                    if pp_free[ppi % 4] is not None:
                        waits.append(pp_free[ppi % 4])
                    pairs = [(wAq(wi, ci)[:, 128 * do:128 * do + 128],
                              xtb[:, WID * ci + s0:WID * ci + s0 + sw])
                             for ci in range(NCIN)]
                    tmm = P.mmchain(bank[:, 0:sw], pairs, waits=waits)
                    pp_free[ppi % 4] = P.act(
                        outb[:, WID * do + s0:WID * do + s0 + sw], bank[:, 0:sw],
                        AF.Identity, bias=b3s[:, 8 * wi + do:8 * wi + do + 1],
                        waits=[tmm])
                    ppi += 1
        for tti in range(6):
            for half in range(2):
                bank = ppb[ppi % 4]
                waits = list(t_xc)
                if pp_free[ppi % 4] is not None:
                    waits.append(pp_free[ppi % 4])
                pairs = [(xtb[:, WID * ci + 128 * tti:WID * ci + 128 * tti + 128],
                          wAq(2, ci)[:, 512 * half:512 * half + 512])
                         for ci in range(NCIN)]
                tmm = P.mmchain(bank[:, 0:512], pairs, waits=waits)
                pp_free[ppi % 4] = P.tt(
                    vt[:, 1024 * tti + 512 * half:1024 * tti + 512 * half + 512],
                    bank[:, 0:512], bvr[:, 512 * half:512 * half + 512],
                    ALU.add, waits=[tmm])
                ppi += 1
        P.barrier()

        mmt = {}
        red_prev = None
        sq_free = {}
        for ti, src in enumerate((qt, kt)):
            for isl, (s0, sw) in enumerate(slabs768):
                last = None
                for ci in range(NCIN):
                    k2 = ci % 2
                    sqk = sqm[:, 512 * k2:512 * k2 + sw]
                    w = [sq_free[k2]] if k2 in sq_free else []
                    t1 = P.act(sqk, src[:, WID * ci + s0:WID * ci + s0 + sw],
                               AF.Square, waits=w)
                    wmm = [t1] + ([red_prev] if (ci == 0 and red_prev) else [])
                    last = P.mm(pz[0:1, 0:sw], obs[:], sqk, start=(ci == 0),
                                stop=(ci == NCIN - 1), waits=wmm)
                    sq_free[k2] = last
                mmt[(ti, isl)] = P.emit("vector", lambda e, sw=sw, ti=ti, isl=isl: e.tensor_reduce(
                    out=mrow[0:1, 3 * ti + isl:3 * ti + isl + 1], in_=pz[0:1, 0:sw],
                    axis=mybir.AxisListType.X, op=ALU.max), waits=[last])
                red_prev = mmt[(ti, isl)]
        t1 = P.tt(mrow[0:1, 6:7], mrow[0:1, 0:1], mrow[0:1, 1:2], ALU.max,
                  waits=list(mmt.values()))
        t2 = P.tt(mrow[0:1, 7:8], mrow[0:1, 3:4], mrow[0:1, 4:5], ALU.max,
                  waits=list(mmt.values()))
        t3 = P.tt(mrow[0:1, 6:7], mrow[0:1, 6:7], mrow[0:1, 7:8], ALU.mult,
                  waits=[t1, t2])
        t4 = P.act(mrow[0:1, 6:7], mrow[0:1, 6:7], AF.Sqrt, waits=[t3])
        t5 = P.ts(mrow[0:1, 6:7], mrow[0:1, 6:7], -INV32, ALU.mult, waits=[t4])
        t6 = P.mm(pzb[:, 0:128][:, 0:1], o1s[:], mrow[0:1, 6:7], start=True,
                  stop=True, waits=[t5])
        P.act(biasM[:], pzb[:, 0:128][:, 0:1], AF.Copy, waits=[t6])
        P.barrier()

        ss_free = expm_free = po_free = pzb_free = osb_free = None
        tzr_prev = tzb_prev = None
        mk_free = {}
        tz0 = P.memset(expm[:, 0:128], 0.0)
        for j in range(-1, 4):
            qa = 128 * (j + 1)
            msl = (j + 1) % 2
            wmk = [mk_free[msl]] if msl in mk_free else []
            tmkl = P.dma(mka[:, 384 * msl:384 * msl + 384], mk_d[i, j + 1],
                         waits=wmk, eng="gpsimd")
            smm = []
            p0 = 0 if j >= 0 else 1
            for p in range(p0, 3):
                kcol = qa + 128 * (p - 1)
                w0 = [ss_free] if ss_free else []
                pairs = [(kt[:, WID * ci + kcol:WID * ci + kcol + 128],
                          qt[:, WID * ci + qa:WID * ci + qa + 128])
                         for ci in range(NCIN)]
                smm.append(P.mmchain(B3[:, 128 * p:128 * p + 128], pairs, waits=w0))
            we = list(smm) + ([expm_free] if expm_free else []) + [tz0]
            te = P.act(expm[:, 128 * p0:384], B3[:, 128 * p0:384], AF.Exp,
                       bias=biasM[:], scale=INV32, waits=we)
            ss_free = te
            tmsk = P.tt(expm[:, 0:384], expm[:, 0:384],
                        mka[:, 384 * msl:384 * msl + 384], ALU.mult,
                        waits=[te, tmkl])
            mk_free[msl] = tmsk
            zmm = None
            for p in range(3):
                w0 = []
                if p == 0:
                    w0 = [tmsk] + ([tzr_prev] if tzr_prev else [])
                zmm = P.mm(pz[0:1, 0:128], obs[:], expm[:, 128 * p:128 * p + 128],
                           start=(p == 0), stop=(p == 2), waits=w0)
            tzr = P.recip(zrA[:], pz[0:1, 0:128],
                          waits=[zmm] + ([tzb_prev] if tzb_prev else []))
            tzr_prev = tzr
            tzb = P.mm(pzb[:, 0:128], o1s[:], zrA[:], start=True, stop=True,
                       waits=[tzr] + ([pzb_free] if pzb_free else []))
            tzb_prev = tzb
            pvt = None
            p_list = [p for p in range(3) if (j + p) >= 0]
            for co in range(8):
                for ip, p in enumerate(p_list):
                    tti = j + p
                    bank = B4 if co < 4 else B5
                    cof = co % 4
                    w0 = []
                    if co == 0 and ip == 0:
                        w0 = [tmsk] + ([po_free] if po_free else [])
                    pvt = P.mm(bank[:, 128 * cof:128 * cof + 128],
                               vt[:, 1024 * tti + 128 * co:1024 * tti + 128 * co + 128],
                               expm[:, 128 * p:128 * p + 128],
                               start=(ip == 0), stop=(ip == len(p_list) - 1), waits=w0)
            expm_free = pvt
            wev = [pvt] + ([osb_free] if osb_free else [])
            tev1 = P.act(o_sb[:, 0:512], B4[:], AF.Copy, waits=wev)
            tev2 = P.act(o_sb[:, 512:1024], B5[:], AF.Copy, waits=wev)
            po_free = tev2
            tmu = P.tt(r3(o_sb, 8, 128), r3(o_sb, 8, 128), bc3(pzb[:, 0:128], 8),
                       ALU.mult, waits=[tev1, tev2, tzb])
            pzb_free = tmu
            xv = r3(xt, NCIN, WID)[:, :, qa:qa + 128]
            tres = P.tt(xv, xv, r3(o_sb, 8, 128), ALU.add, waits=[tmu])
            osb_free = tres
        P.barrier()

        def dst_x1(oc, sw, t12, i=i):
            td = None
            for ci in range(NCIN):
                td = P.dma(x1_d[i, 128 * ci:128 * ci + 128, oc:oc + sw],
                           t_sbA[:, 512 * ci:512 * ci + sw], waits=[t12], eng="gpsimd")
            return td
        _ln_fm(P, T, xt, WID, [(127, 512, 0), (128, 512, 1)], dst_x1, {})
        P.barrier()
    P.barrier()

    x1B = at_("x1B", 0, [128, NCIN * XW], F32)
    dxt = at_("dxtB", 16416, [128, 4096], F32)
    xrb = at_("xrbB", 32800, [128, 4096], BF16)
    xkb = at_("xkbB", 40992, [128, 4096], BF16)
    wB = at_("wB", 49184, [128, 3 * NCIN * 1024], BF16)
    r_bf = at_("rbfB", 98336, [128, 4096], BF16)
    ka_bf = at_("kabB", 106528, [128, 4096], BF16)
    vsbB = at_("vsbB", 114720, [128, 512], F32)
    x2B = at_("x2B", 116768, [128, 4096], F32)
    lnrbB = at_("lnrbB", 133152, [128, 4096], BF16)
    lnsqB = at_("lnsqB", 141344, [128, 4096], BF16)
    wstB = at_("wstB", 149536, [128, 1024], F32)
    wstB2 = at_("wstB2", 153632, [128, 1024], F32)
    wBq = lambda wi, ci: wB[:, 1024 * (NCIN * wi + ci):1024 * (NCIN * wi + ci) + 1024]

    stgB_last = [None, None]
    cntB = 0
    for wi, wd in enumerate((wr_d, wk2_d, wv2_d)):
        for ci in range(NCIN):
            par = cntB % 2
            cntB += 1
            stg = (wstB, wstB2)[par]
            tl = pool_w.dma(stg, wd[128 * ci:128 * ci + 128, :],
                            waits=([stgB_last[par]] if stgB_last[par] else []))
            stgB_last[par] = P.copy(wBq(wi, ci), stg, waits=[tl])
    P.barrier()
    T["lnrb"] = lnrbB
    T["lnsq"] = lnsqB
    T["lnout"] = x2B

    for i in range(4):
        t_xl = [pool_x.dma(x1B[:, XW * ci:XW * ci + XW], x1_d[i, 128 * ci:128 * ci + 128, :])
                for ci in range(NCIN)]
        t_pm = P.dma(wstB[:, 0:1], pm_d[i], waits=[t_xl[-1]])
        c0v = r3(x1B, NCIN, XW)[:, :, 0:1]
        t_z0 = P.ts(c0v, c0v, wstB[:, 0:1], ALU.mult, waits=[t_pm] + t_xl)
        t_mix = []
        for ci in range(NCIN):
            xc = x1B[:, XW * ci + 1:XW * ci + XW]
            xp = x1B[:, XW * ci:XW * ci + XW - 1]
            td = P.tt(dxt[:, 512 * ci:512 * ci + 512], xc, xp, ALU.subtract,
                      waits=[t_z0])
            t1m = P.stt(xrb[:, 512 * ci:512 * ci + 512], dxt[:, 512 * ci:512 * ci + 512],
                        mus[:, ci:ci + 1], xp, ALU.mult, ALU.add, waits=[td])
            t2m = P.stt(xkb[:, 512 * ci:512 * ci + 512], dxt[:, 512 * ci:512 * ci + 512],
                        mus[:, 8 + ci:8 + ci + 1], xp, ALU.mult, ALU.add, waits=[td])
            t_mix += [t1m, t2m]
        ppi = 0
        ppb = [B0, B1, B2, B3]
        pp_free = [None, None, None, None]
        proj_last = None
        for wi, (inb, outb, fn) in enumerate(((xrb, r_bf, AF.Sigmoid),
                                              (xkb, ka_bf, AF.Relu))):
            for do in range(NCIN):
                bank = ppb[ppi % 4]
                waits = list(t_mix)
                if pp_free[ppi % 4] is not None:
                    waits.append(pp_free[ppi % 4])
                pairs = [(wBq(wi, ci)[:, 128 * do:128 * do + 128],
                          inb[:, 512 * ci:512 * ci + 512]) for ci in range(NCIN)]
                tmm = P.mmchain(bank[:, 0:512], pairs, waits=waits)
                te = P.act(outb[:, 512 * do:512 * do + 512], bank[:, 0:512], fn,
                           waits=[tmm])
                pp_free[ppi % 4] = te
                proj_last = te
                ppi += 1
        tsq2 = P.act(ka_bf, ka_bf, AF.Square, waits=[proj_last])
        vfree = None
        for do in range(NCIN):
            bank = ppb[ppi % 4]
            waits = [tsq2]
            if pp_free[ppi % 4] is not None:
                waits.append(pp_free[ppi % 4])
            pairs = [(wBq(2, ci)[:, 128 * do:128 * do + 128],
                      ka_bf[:, 512 * ci:512 * ci + 512]) for ci in range(NCIN)]
            tmm = P.mmchain(bank[:, 0:512], pairs, waits=waits)
            wv_ = [tmm] + ([vfree] if vfree else [])
            tev = P.act(vsbB, bank[:, 0:512], AF.Copy, waits=wv_)
            pp_free[ppi % 4] = tev
            tm2 = P.tt(vsbB, vsbB, r_bf[:, 512 * do:512 * do + 512], ALU.mult,
                       waits=[tev])
            xs_ = x1B[:, XW * do + 1:XW * do + 1 + 512]
            tr2 = P.tt(xs_, xs_, vsbB, ALU.add, waits=[tm2])
            vfree = tr2
            ppi += 1
        P.barrier()

        def dst_x2(oc, sw, t12, i=i):
            td = None
            for ci in range(NCIN):
                td = P.dma(x2_d[i, 128 * ci:128 * ci + 128, oc:oc + sw],
                           x2B[:, 512 * ci:512 * ci + sw], waits=[t12], eng="gpsimd")
            return td
        _ln_fm(P, T, x1B, XW, [(1, 512, 0)], dst_x2, {})
        P.barrier()
    P.barrier()

    x2C = at_("x2C", 0, [128, 4096], F32)
    x2b = at_("x2bC", 16384, [128, 4096], BF16)
    wqkv = at_("wqkvC", 24576, [128, 3072], BF16)
    wstC = at_("wstC", 30720, [128, 1024], F32)
    cssC = at_("cssC", 34816, [32, 512], F32)
    snsC = at_("snsC", 36864, [32, 512], F32)
    qfC = at_("qfC", 38912, [128, 512], F32)
    kfC = at_("kfC", 40960, [128, 512], F32)
    raC = at_("raC", 43008, [32, 512], F32)
    rbC = at_("rbC", 45056, [32, 512], F32)
    qoddC = at_("qoddC", 47104, [32, 512], F32)
    wstC2 = at_("wstC2", 57344, [128, 1024], F32)
    re_tC = at_("re_tC", 49152, [32, 512], BF16)
    ro_tC = at_("ro_tC", 50176, [32, 512], BF16)
    skpC = at_("skpC", 51200, [128, 512], BF16)
    vtileC = at_("vtileC", 52224, [128, 128], BF16)
    sqC = at_("sqC", 52480, [128, 512], BF16)
    knmC = at_("knmC", 53504, [1, 32], F32)

    stgC_cnt = 0
    stgC_last = [None, None]
    for i in range(4):
        t_x2l = []
        for ci in range(NCIN):
            tl = pool_x.dma(x2C[:, 512 * ci:512 * ci + 512],
                            x2_d[i, 128 * ci:128 * ci + 128, :])
            t_x2l.append(tl)
        t_x2b = P.copy(x2b, x2C, waits=t_x2l)
        t_cs = P.dma(cssC[:], cs_d[i])
        t_sn = P.dma(snsC[:], sn_d[i])
        prev = {}
        for h in range(NH):
            tags_w = []
            for wi, wd in enumerate((wq1_d, wk1_d, wv1_d)):
                par = stgC_cnt % 2
                stgC_cnt += 1
                stg = (wstC, wstC2)[par]
                tl = pool_w.dma(r3(stg, NCIN, 128),
                                wd[h].rearrange("(c p) d -> p c d", p=128),
                                waits=([stgC_last[par]] if stgC_last[par] else []))
                wcp = [tl]
                if wi == 0 and prev.get("wq"):
                    wcp.append(prev["wq"])
                tcw = P.copy(wqkv[:, 1024 * wi:1024 * wi + 1024], stg, waits=wcp)
                stgC_last[par] = tcw
                tags_w.append(tcw)
            for ti, ff in ((0, qfC), (1, kfC)):
                bank = pp[ti]
                w0 = [t_x2b] + tags_w + ([prev[("pp", ti)]] if ("pp", ti) in prev else [])
                pairs = [(wqkv[:, 1024 * ti + 128 * ci:1024 * ti + 128 * ci + 128],
                          x2b[:, 512 * ci:512 * ci + 512]) for ci in range(NCIN)]
                tmm = P.mmchain(bank[:, 0:512], pairs, waits=w0)
                wtv = [tmm] + [prev[k] for k in (("mv", ti), ("ff", ti)) if k in prev]
                tev = P.act(ff, bank[:, 0:512], AF.Copy, waits=wtv)
                prev[("pp", ti)] = tev
                wo_ = [prev[("o",)]] if ("o",) in prev else []
                wrp = [prev[("rope",)]] if ("rope",) in prev else []
                wod = [prev[("odd",)]] if ("odd",) in prev else []
                tmv = P.dma(qoddC, ff[32:64, :], waits=[tev] + wod)
                a1 = P.tt(raC, ff[0:32, :], cssC[:], ALU.mult,
                          waits=[tev, t_cs, t_sn] + wo_ + wrp)
                a2 = P.tt(rbC, qoddC, snsC[:], ALU.mult,
                          waits=[tev, tmv, t_cs, t_sn] + wo_ + wrp)
                r1 = P.tt(re_tC, raC, rbC, ALU.subtract,
                          waits=[a1, a2] + ([prev[("d1",)]] if ("d1",) in prev else []))
                a3 = P.tt(raC, ff[0:32, :], snsC[:], ALU.mult, waits=[r1])
                a4 = P.tt(rbC, qoddC, cssC[:], ALU.mult, waits=[r1])
                r2 = P.tt(ro_tC, raC, rbC, ALU.add,
                          waits=[a3, a4] + ([prev[("d2",)]] if ("d2",) in prev else []))
                r3c = P.copy(skpC[64:128, :], ff[64:128, :], waits=[tev] + wo_)
                if ti == 0:
                    d1 = P.dma(qo_d[i, h, 0:32, :], re_tC, waits=[r1], eng="gpsimd")
                    d2 = P.dma(qo_d[i, h, 32:64, :], ro_tC, waits=[r2], eng="gpsimd")
                    d3 = P.dma(qo_d[i, h, 64:128, :], skpC[64:128, :], waits=[r3c], eng="gpsimd")
                else:
                    kv0 = kvin.ap()
                    d1 = P.dma(kv0[0, h, i, 0:32, :], re_tC, waits=[r1], eng="gpsimd")
                    d2 = P.dma(kv0[0, h, i, 32:64, :], ro_tC, waits=[r2], eng="gpsimd")
                    d3 = P.dma(kv0[0, h, i, 64:128, :], skpC[64:128, :], waits=[r3c], eng="gpsimd")
                prev[("o",)] = d3
                prev[("d1",)] = d1
                prev[("d2",)] = d2
                prev[("mv", ti)] = tmv
                prev[("odd",)] = a4
                prev[("rope",)] = r2
                prev[("ff", ti)] = P.esem["vector"].tag()
                if ti == 1:
                    wn = [tev] + ([prev[("sqk",)]] if ("sqk",) in prev else [])
                    tq1 = P.act(sqC, ff, AF.Square, waits=wn)
                    nmm = P.mm(pz[0:1, 0:512], obs[:], sqC, start=True, stop=True,
                               waits=[tq1] + ([prev[("nred",)]] if ("nred",) in prev else []))
                    prev[("sqk",)] = nmm
                    tnr = P.emit("vector", lambda e, h=h: e.tensor_reduce(
                        out=knmC[0:1, h:h + 1], in_=pz[0:1, 0:512],
                        axis=mybir.AxisListType.X, op=ALU.max),
                        waits=[nmm] + ([prev[("nuse",)]] if ("nuse",) in prev else []))
                    prev[("nred",)] = tnr
            for tt_ in range(4):
                bank = B3
                w0 = [t_x2b] + tags_w + ([prev[("pv",)]] if ("pv",) in prev else [])
                pairs = [(x2b[:, 512 * ci + 128 * tt_:512 * ci + 128 * tt_ + 128],
                          wqkv[:, 2048 + 128 * ci:2048 + 128 * ci + 128])
                         for ci in range(NCIN)]
                tmm = P.mmchain(bank[:, 0:128], pairs, waits=w0)
                wv2 = [tmm] + ([prev[("vo",)]] if ("vo",) in prev else [])
                tev = P.act(vtileC, bank[:, 0:128], AF.Copy, waits=wv2)
                prev[("pv",)] = tev
                dv = P.dma(kvin.ap()[1, h, i, :, 128 * tt_:128 * tt_ + 128],
                           vtileC, waits=[tev], eng="gpsimd")
                prev[("vo",)] = dv
            prev["wq"] = P.esem["tensor"].tag()
        tcp2 = P.copy(mrow[0:1, 16 + 8 * i:24 + 8 * i], knmC[0:1, 0:8],
                      waits=[prev[("nred",)]] + ([prev.get(("nuse2",))] if ("nuse2",) in prev else []))
        prev[("nuse",)] = tcp2
        P.barrier()
    tnr1 = P.copy(nr4[0:1, 0:32], mrow[0:1, 16:48])
    tnr2 = P.dma(nrin.ap()[0:1, :], nr4[0:1, 0:32], waits=[tnr1], eng="gpsimd")
    P.barrier()

    ccs1 = P.sem("cc1")
    ccs1b = P.sem("cc1b")
    def cc1(e, s=ccs1, s2=ccs1b):
        e.collective_compute(
            "AllGather", mybir.AluOpType.bypass,
            replica_groups=[[0, 2, 4, 6], [1, 3, 5, 7]],
            ins=[kvin.ap().opt()], outs=[kvg.ap().opt()],
        ).then_inc(s.h, 1)
        e.collective_compute(
            "AllGather", mybir.AluOpType.bypass,
            replica_groups=[[0, 2, 4, 6], [1, 3, 5, 7]],
            ins=[nrin.ap().opt()], outs=[nrg.ap().opt()],
        ).then_inc(s2.h, 1)
    P.raw("gpsimd", cc1)
    ccs1.n += 1
    ccs1b.n += 1
    t_cc1 = (ccs1, 1)
    t_cc1b = (ccs1b, 1)
    P.barrier()

    qbP = at_("qbP", 0, [128, 512], BF16)
    kbP = at_("kbP", 1024, [128, 7680], BF16)
    vbP = at_("vbP", 16384, [128, 7680], BF16)
    dkbP = at_("dkbP", 31744, [128, 512], BF16)
    dvbP = at_("dvbP", 32768, [128, 512], BF16)
    atP = at_("atP", 33792, [128, 4096], BF16)
    tmP = at_("tmP", 41984, [128, 64], F32)
    baP = at_("baP", 42240, [128, 64], F32)
    expbP = at_("expbP", 42496, [128, 2048], BF16)
    zrP = at_("zrP", 120128, [1, 512], F32)
    osbP = at_("osbP", 46592, [128, 512], F32)
    woBP = at_("woBP", 48640, [128, 8192], BF16)
    x2P = at_("x2P", 65024, [128, 4096], F32)
    x3P = at_("x3P", 81408, [128, 4096], F32)
    lnrbP = at_("lnrbP", 97792, [128, 4096], BF16)
    lnsqP = at_("lnsqP", 105984, [128, 4096], BF16)
    wstP = at_("wstP", 114176, [128, 1024], F32)
    sqP = at_("sqP", 118272, [128, 512], BF16)
    bMh = at_("bMh", 119296, [128, 8], F32)
    trtP = at_("trtP", 119360, [128, 128], F32)
    trtPb = at_("trtPb", 119872, [128, 128], BF16)
    idIs = idIsp[:]
    T["lnrb"] = lnrbP
    T["lnsq"] = lnsqP
    T["lnout"] = x3P

    t_idI = P.dma(idIs, idI_d)
    t_n4 = P.dma(nr4[0:1, :], nrg.ap().rearrange("r k -> (r k)").unsqueeze(0), waits=[t_cc1b])
    t_mk2 = P.emit("vector", lambda e: e.tensor_reduce(
        out=maxk2[0:1, :],
        in_=nr4[0:1, :].rearrange("p (r i h) -> p h r i", r=4, i=4, h=8),
        axis=mybir.AxisListType.XY, op=ALU.max), waits=[t_n4])
    wstP2 = at_("wstP2", 122176, [128, 1024], F32)
    stgP_last = [None, None]
    for ci in range(NCIN):
        par = ci % 2
        stg = (wstP, wstP2)[par]
        tl = pool_w.dma(stg, wo_d[128 * ci:128 * ci + 128, :],
                        waits=([stgP_last[par]] if stgP_last[par] else []))
        stgP_last[par] = P.copy(woBP[:, 1024 * ci:1024 * ci + 1024], stg, waits=[tl])
    P.barrier()

    for i in range(4):
        Pi = PEXT[i]
        nkr = Pi - 4
        t_tm = P.dma(tmP, tm_d[i])
        t_x2 = [pool_x.dma(x2P[:, 512 * ci:512 * ci + 512],
                           x2_d[i, 128 * ci:128 * ci + 128, :]) for ci in range(NCIN)]
        prev = {}
        at_done = []
        for h in range(NH):
            t_q = P.dma(qbP, qo_d[i, h], waits=[t for t in (prev.get("qfree"),) if t])
            wq1_ = [t_q] + [t for t in (prev.get("sqP"),) if t]
            tq1 = P.act(sqP, qbP, AF.Square, waits=wq1_)
            nmm = P.mm(pz[0:1, 0:512], obs[:], sqP, start=True, stop=True,
                       waits=[tq1] + [t for t in (prev.get("pzR"),) if t])
            prev["sqP"] = nmm
            tnr = P.emit("vector", lambda e: e.tensor_reduce(
                out=mrow[0:1, 0:1], in_=pz[0:1, 0:512],
                axis=mybir.AxisListType.X, op=ALU.max),
                waits=[nmm] + [t for t in (prev.get("mr0"),) if t])
            tb1 = P.tt(mrow[0:1, 1:2], mrow[0:1, 0:1], maxk2[0:1, h:h + 1], ALU.mult,
                       waits=[tnr, t_mk2] + [t for t in (prev.get("mr1"),) if t])
            tb2 = P.act(mrow[0:1, 1:2], mrow[0:1, 1:2], AF.Sqrt, waits=[tb1])
            tb3 = P.ts(mrow[0:1, 1:2], mrow[0:1, 1:2], -INV128, ALU.mult, waits=[tb2])
            prev["mr0"] = tb3
            tb4 = P.mm(pzb[:, 0:128][:, 0:1], o1s[:], mrow[0:1, 1:2], start=True,
                       stop=True, waits=[tb3] + [t for t in (prev.get("zbuse"),) if t])
            tb5 = P.act(bMh[:, 0:1], pzb[:, 0:128][:, 0:1], AF.Copy,
                        waits=[tb4] + [t for t in (prev.get("bMuse"),) if t])
            prev["mr1"] = tb5
            tba = P.ts(baP, tmP, bMh[:, 0:1], ALU.add,
                       waits=[tb5, t_tm] + [t for t in (prev.get("bause"),) if t])
            prev["bMuse"] = tba
            t_kc = {}
            t_vc = {}
            wkv = [t_cc1] + [t for t in (prev.get("kvfree"),) if t]
            for g in range(nkr // 4):
                t_kc[g] = pool_kv.dma(kbP[:, 512 * g:512 * g + 512],
                                kvg.ap()[g % 4, 0, h, g // 4], waits=wkv)
                t_vc[g] = pool_kv.dma(vbP[:, 512 * g:512 * g + 512],
                                kvg.ap()[g % 4, 1, h, g // 4], waits=wkv)
            t_dk = pool_kv.dma(dkbP, kvin.ap()[0, h, i], waits=wkv)
            t_dv = pool_kv.dma(dvbP, kvin.ap()[1, h, i], waits=wkv)
            t_exp = {}
            pvt = zmm = None
            sbanks = [pp[0], pp[1], B2, B5]
            nit = nkr + 4
            for kt in range(nit):
                sl = kt % 4
                diag = kt >= nkr
                jt = kt - nkr
                if diag:
                    lhs_s = dkbP[:, 128 * jt:128 * jt + 128]
                    lhs_v = dvbP[:, 128 * jt:128 * jt + 128]
                    wsrc = [t_dk]
                    wvsrc = [t_dv]
                else:
                    lhs_s = kbP[:, 128 * kt:128 * kt + 128]
                    lhs_v = vbP[:, 128 * kt:128 * kt + 128]
                    wsrc = [t_kc[kt // 4]]
                    wvsrc = [t_vc[kt // 4]]
                w0 = wsrc + [t_q, tba] + ([t_exp[sl]] if sl in t_exp else [])
                tsm = P.mm(sbanks[sl][:], lhs_s, qbP, start=True, stop=True, waits=w0)
                we = [tsm, tba]
                if kt >= 4:
                    we.append(prev.get(("euse", sl)))
                te = P.act(expbP[:, 512 * sl:512 * sl + 512], sbanks[sl][:], AF.Exp,
                           bias=(bMh[:, 0:1] if diag else baP[:, kt:kt + 1]),
                           scale=INV128, waits=[w for w in we if w])
                tin = te
                if diag:
                    tin = P.tt(expbP[:, 512 * sl:512 * sl + 512],
                               expbP[:, 512 * sl:512 * sl + 512],
                               cms[:, 512 * jt:512 * jt + 512], ALU.mult, waits=[te])
                t_exp[sl] = tin
                wp = [tin] + wvsrc + ([prev.get("poB3")] if (kt == 0 and "poB3" in prev) else [])
                pvt = P.mm(B3[:], lhs_v, expbP[:, 512 * sl:512 * sl + 512],
                           start=(kt == 0), stop=(kt == nit - 1),
                           waits=[w for w in wp if w])
                wz = [tin] + ([prev.get("pzR")] if (kt == 0 and "pzR" in prev) else [])
                zmm = P.mm(pz[0:1, 0:512], obs[:], expbP[:, 512 * sl:512 * sl + 512],
                           start=(kt == 0), stop=(kt == nit - 1),
                           waits=[w for w in wz if w])
                prev[("euse", sl)] = zmm
            tzr = P.recip(zrP[:], pz[0:1, 0:512],
                          waits=[zmm] + [t for t in (prev.get("zruse"),) if t])
            prev["pzR"] = tzr
            tzb = P.mm(B4[:], o1s[:], zrP[:], start=True, stop=True,
                       waits=[tzr] + [t for t in (prev.get("B4use"),) if t])
            prev["zruse"] = tzb
            tev = P.act(osbP, B3[:], AF.Copy,
                        waits=[pvt] + [t for t in (prev.get("osb"),) if t])
            prev["poB3"] = tev
            tmu = P.tt(atP[:, 512 * h:512 * h + 512], osbP, B4[:], ALU.mult,
                       waits=[tev, tzb] + [t for t in (prev.get("atuse"),) if t])
            prev["B4use"] = tmu
            prev["osb"] = tmu
            prev["zbuse"] = tmu
            prev["bause"] = P.esem["scalar"].tag()
            prev["qfree"] = P.esem["tensor"].tag()
            prev["kvfree"] = prev["qfree"]
            at_done.append(tmu)
        vfree = None
        pp_free = [None, None]
        for do in range(NCIN):
            bank = pp[do % 2]
            waits = list(at_done)
            if pp_free[do % 2] is not None:
                waits.append(pp_free[do % 2])
            pairs = [(woBP[:, 1024 * h + 128 * do:1024 * h + 128 * do + 128],
                      atP[:, 512 * h:512 * h + 512]) for h in range(NH)]
            tmm = P.mmchain(bank[:, 0:512], pairs, waits=waits)
            wv_ = [tmm] + ([vfree] if vfree else [])
            tev = P.act(osbP, bank[:, 0:512], AF.Copy, waits=wv_)
            pp_free[do % 2] = tev
            xs_ = x2P[:, 512 * do:512 * do + 512]
            tr2 = P.tt(xs_, xs_, osbP, ALU.add, waits=[tev] + t_x2)
            vfree = tr2
        P.barrier()

        def dst_x3(oc, sw, t12, i=i):
            td = None
            for ci in range(NCIN):
                td = P.dma(x3_d[i, 128 * ci:128 * ci + 128, oc:oc + sw],
                           x3P[:, 512 * ci:512 * ci + sw], waits=[t12], eng="gpsimd")
            return td
        _ln_fm(P, T, x2P, 512, [(0, 512, 0)], dst_x3, {})
        P.barrier()
        pu = {}
        for ci in range(NCIN):
            for tq in range(4):
                w0 = [t_idI] + [t for t in (pu.get("ps"),) if t]
                ttr = P.transpose(B5[:, 0:128],
                                  x3P[:, 512 * ci + 128 * tq:512 * ci + 128 * tq + 128],
                                  idIs, waits=w0)
                tevt = P.act(trtP, B5[:, 0:128], AF.Copy,
                             waits=[ttr] + [t for t in (pu.get("tr"), pu.get("cb")) if t])
                pu["ps"] = tevt
                tcb = P.copy(trtPb, trtP, waits=[tevt] + [t for t in (pu.get("trb"),) if t])
                pu["cb"] = tcb
                r0_ = 512 * i + 128 * tq
                tdt = P.dma(x3r_d.ap()[r0_:r0_ + 128, 128 * ci:128 * ci + 128],
                            trtP, waits=[tevt], eng="gpsimd")
                tdb = P.dma(x3rb_d.ap()[r0_:r0_ + 128, 128 * ci:128 * ci + 128],
                            trtPb, waits=[tcb], eng="gpsimd")
                pu["tr"] = tdt
                pu["trb"] = tdb
        P.barrier()
    P.barrier()

    x3bM = at_("x3bM", 0, [128, 8192], BF16)
    accM = at_("accM", 16384, [128, 8192], F32)
    htM = at_("htM", 49152, [128, 32768], BF16)
    stgM = [at_("stgM0", 114688, [128, 1024], F32),
            at_("stgM1", 118784, [128, 1024], F32),
            at_("stgM2", 78016, [128, 1024], F32),
            at_("stgM3", 82112, [128, 1024], F32)]
    w1bM = [at_("w1bM0", 122880, [128, 1024], BF16),
            at_("w1bM1", 124928, [128, 1024], BF16),
            at_("w1bM2", 86208, [128, 1024], BF16),
            at_("w1bM3", 88256, [128, 1024], BF16)]
    w2bM = [at_("w2bM0", 126976, [128, 512], BF16),
            at_("w2bM1", 128000, [128, 512], BF16),
            at_("w2bM2", 90304, [128, 512], BF16),
            at_("w2bM3", 91328, [128, 512], BF16)]
    ygs2 = [at_("ygs0", 63488, [128, 512], F32),
            at_("ygs1", 92352, [128, 512], F32)]
    scfsM = at_("scfsM", 129024, [128, 512], F32)
    mskM = at_("mskM", 131072, [128, 512], BF16)
    rowsumM = at_("rowsumM", 132096, [128, 1], F32)
    taubM = at_("taubM", 132128, [128, 1], F32)
    b1sM = at_("b1sM", 132160, [128, 256], F32)
    b2sM = at_("b2sM", 133184, [8, 1024], F32)
    sqtM = at_("sqtM", 137280, [128, 1024], F32)
    x4tM = at_("x4tM", 141376, [128, 1024], F32)
    mrow4 = at_("mrow4", 145472, [128, 4], F32)
    rwbM = at_("rwbM", 145504, [128, 64], BF16)
    lgM = at_("lgM", 145664, [8, 512], F32)
    ltmM = at_("ltmM", 147712, [128, 8], F32)
    mxM = at_("mxM", 147744, [128, 1], F32)
    smM = at_("smM", 147776, [128, 1], F32)
    sctM = at_("sctM", 147808, [128, 8], F32)
    scTM = at_("scTM", 147840, [8, 512], F32)

    t_b1l = P.dma(b1sM, b1_d)
    t_b2l = P.dma(b2sM[:, :], b2_d)
    t_rw = P.dma(r3(wstP[:, 0:64], NCIN, 8), rw_d.rearrange("(c p) e -> p c e", p=128))
    t_rwb = P.copy(rwbM[:, 0:64], wstP[:, 0:64], waits=[t_rw])
    pu = {}
    sc_dma = []
    for i in range(4):
        t_xl = []
        for ci in range(NCIN):
            k = ci % 2
            w0 = [t_xl[ci - 2]] if ci >= 2 else [t_rwb] + [t for t in (pu.get("stg"),) if t]
            tl = P.dma(stgM[k][:, 0:512], x3_d[i, 128 * ci:128 * ci + 128, :], waits=w0)
            t_xl.append(P.copy(x3bM[:, 512 * ci:512 * ci + 512], stgM[k][:, 0:512],
                               waits=[tl] + [t for t in (pu.get("x3b"),) if t]))
        pu["stg"] = t_xl[-1]
        ev_prev = pu.get("lg")
        pairs = [(rwbM[:, 8 * ci:8 * ci + 8], x3bM[:, 512 * ci:512 * ci + 512])
                 for ci in range(NCIN)]
        tmm = P.mmchain(B3[0:8, :], pairs,
                        waits=t_xl + [t_rwb] + ([ev_prev] if ev_prev else []))
        tevl = P.act(lgM[:, :], B3[0:8, :], AF.Copy, waits=[tmm])
        pu["lg"] = tevl
        pu["x3b"] = tmm
        for tt in range(4):
            w0 = [tevl] + [t for t in (pu.get("pt"),) if t]
            ttr = P.transpose(B5[0:128, 0:8], lgM[:, 128 * tt:128 * tt + 128],
                              id8s[:], waits=w0)
            tevt = P.act(ltmM, B5[0:128, 0:8], AF.Copy,
                         waits=[ttr] + [t for t in (pu.get("ltm"),) if t])
            pu["pt"] = tevt
            tmx = P.emit("vector", lambda e: e.tensor_reduce(
                out=mxM[:], in_=ltmM, axis=mybir.AxisListType.X, op=ALU.max),
                waits=[tevt] + [t for t in (pu.get("mx"),) if t])
            tsb = P.ts(sctM, ltmM, mxM[:, 0:1], ALU.subtract,
                       waits=[tmx] + [t for t in (pu.get("sct"),) if t])
            pu["mx"] = tsb
            pu["ltm"] = tsb
            tex = P.act(sctM, sctM, AF.Exp, waits=[tsb])
            tsm = P.emit("vector", lambda e: e.tensor_reduce(
                out=smM[:], in_=sctM, axis=mybir.AxisListType.X, op=ALU.add),
                waits=[tex] + [t for t in (pu.get("sm"),) if t])
            trc = P.recip(smM[:], smM[:], waits=[tsm])
            tml = P.ts(sctM, sctM, smM[:, 0:1], ALU.mult, waits=[trc])
            pu["sm"] = tml
            w1_ = [tml, t_idI] + [t for t in (pu.get("pt2"),) if t]
            ttr2 = P.transpose(B4[0:8, 0:128], sctM, idIs, waits=w1_)
            tev2 = P.act(scTM[:, 128 * tt:128 * tt + 128], B4[0:8, 0:128], AF.Copy,
                         waits=[ttr2] + [t for t in (pu.get("scT"), pu.get("scd")) if t])
            pu["pt2"] = tev2
            pu["scT"] = tev2
            pu["sct"] = ttr2
        tds = P.dma(scin.ap()[i], scTM, waits=[pu["scT"]], eng="gpsimd")
        tcs = P.copy(scoTs[:, 512 * i:512 * i + 512], scTM, waits=[pu["scT"]])
        pu["scT"] = tcs
        pu["scd"] = tds
        sc_dma.append(tds)
    P.barrier()
    ccs2 = P.sem("cc2")
    def cc2(e, s=ccs2):
        e.collective_compute(
            "AllGather", mybir.AluOpType.bypass,
            replica_groups=[[0, 2, 4, 6], [1, 3, 5, 7]],
            ins=[scin.ap().opt()], outs=[scg.ap().opt()],
        ).then_inc(s.h, 1)
    P.raw("gpsimd", cc2)
    ccs2.n += 1
    t_cc2 = (ccs2, 1)
    P.barrier()
    t_scf = []
    for r4 in range(4):
        for i2 in range(4):
            m = 4 * i2 + r4
            t_scf.append(P.dma(scfsM[8 * m:8 * m + 8, :], scg.ap()[r4, i2],
                               waits=[t_cc2]))
    tm0 = P.memset(lohi[:, 0:1], 0.0)
    tm1 = P.memset(lohi[:, 1:2], 1.0)
    tm2 = P.memset(lohi[:, 2:3], 0.5)
    pu = {}
    for it in range(NITER):
        tbc = P.mm(B3[:, 0:1], fTs[:], lohi[:, 2:3], start=True, stop=True,
                   waits=[tm2] + [t for t in (pu.get("mid"), pu.get("tbev")) if t])
        tbe = P.act(taubM, B3[:, 0:1], AF.Copy, waits=[tbc])
        pu["tbev"] = tbe
        tms = P.ts(mskM, scfsM, taubM[:, 0:1], ALU.is_ge, s2=0.0, op1=ALU.add,
                   accum_out=rowsumM[:, 0:1],
                   waits=[tbe] + t_scf + [t for t in (pu.get("cnt"),) if t])
        tcc = P.mm(B4[0:8, 0:1], fcs[:], rowsumM[:, 0:1], start=True, stop=True,
                   waits=[tms] + [t for t in (pu.get("cmp"),) if t])
        pu["cnt"] = tcc
        tcp = P.ts(cmpf[:, 0:1], B4[0:8, 0:1], CAP, ALU.is_ge,
                   waits=[tcc] + [t for t in (pu.get("sel2"),) if t])
        pu["cmp"] = tcp
        td1 = P.tt(lohi[:, 3:4], lohi[:, 2:3], lohi[:, 0:1], ALU.subtract,
                   waits=[tcp, tm0, tm1])
        ts1 = P.stt(lohi[:, 0:1], lohi[:, 3:4], cmpf[:, 0:1], lohi[:, 0:1],
                    ALU.mult, ALU.add, waits=[td1])
        td2 = P.tt(lohi[:, 3:4], lohi[:, 1:2], lohi[:, 2:3], ALU.subtract, waits=[ts1])
        ts2 = P.stt(lohi[:, 1:2], lohi[:, 3:4], cmpf[:, 0:1], lohi[:, 2:3],
                    ALU.mult, ALU.add, waits=[td2])
        pu["sel2"] = ts2
        ta = P.tt(lohi[:, 3:4], lohi[:, 0:1], lohi[:, 1:2], ALU.add, waits=[ts1, ts2])
        pu["mid"] = P.ts(lohi[:, 2:3], lohi[:, 3:4], 0.5, ALU.mult, waits=[ta])
    tau_done = pu["mid"]
    gt = {}
    for i in range(4):
        t_g1 = P.ts(gmT[:, 512 * i:512 * i + 512], scoTs[:, 512 * i:512 * i + 512],
                    lohi[:, 0:1], ALU.is_ge, waits=[tau_done])
        t_g2 = P.tt(gmT[:, 512 * i:512 * i + 512], gmT[:, 512 * i:512 * i + 512],
                    scoTs[:, 512 * i:512 * i + 512], ALU.mult, waits=[t_g1])
        for tt in range(4):
            w0 = [t_g2, t_id8z] if False else [t_g2]
            ttr = P.transpose(B5[0:128, 0:8],
                              gmT[:, 512 * i + 128 * tt:512 * i + 128 * tt + 128],
                              id8s[:], waits=w0 + [t for t in (gt.get("pt"),) if t])
            tevt = P.act(g_tm[:, 32 * i + 8 * tt:32 * i + 8 * tt + 8], B5[0:128, 0:8],
                         AF.Copy, waits=[ttr] + [t for t in (gt.get("g"),) if t])
            gt["pt"] = tevt
            gt["g"] = tevt
    t_gtm = gt["g"]
    P.barrier()

    NCAP = 768
    xg_tm = at_("xg_tm", 0, [128, 1024], BF16)
    xg_f = at_("xg_f", 2048, [128, NCIN * NCAP], BF16)
    htG = at_("htG", 14336, [128, NHC * NCAP], BF16)
    ygsb = at_("ygsb", 63488, [128, 512], F32)
    a_tm = at_("a_tm", 65536, [128, 1024], F32)
    m16f = at_("m16f", 69632, [128, 16], F32)
    m16b = at_("m16b", 69696, [128, 16], BF16)
    pfs = at_("pfs", 69760, [128, 16], F32)
    tots = at_("tots", 69824, [16, 1], F32)
    offs = at_("offs", 69856, [16, 4], F32)
    offrow = at_("offrow", 69920, [1, 16], F32)
    dsb = at_("dsb", 69984, [128, 16], F32)
    ddi = at_("ddi", 70048, [128, 16], mybir.dt.int32)
    ixsb = at_("ixsb", 70112, [128, 6], mybir.dt.int32)
    ggsb = at_("ggsb", 70144, [128, 6], F32)
    tiosb = at_("tiosb", 70176, [128, 16], mybir.dt.int32)
    c2ksb = at_("c2ksb", 70240, [1, 768], mybir.dt.int32)
    trlsb = at_("trlsb", 73312, [128, 128], BF16)
    ts16s = at_("ts16s", 73568, [16, 16], F32)
    idIbs = at_("idIbs", 73632, [128, 128], BF16)
    zrow = at_("zrow", 73888, [8, 1024], F32)

    t_tio = P.dma(tiosb, tio_d)
    t_c2k = P.dma(c2ksb, c2k_d)
    t_trl = P.dma(trlsb, trl_d)
    t_t16 = P.dma(ts16s, t16_d)
    t_idb = P.dma(idIbs, idb_d)
    t_zr = P.memset(zrow, 0.0)
    t_zrb = P.memset(zrow.bitcast(BF16)[0:1, 0:1024], 0.0, waits=[t_zr])
    t_z1 = P.dma(x3rb_d.ap()[2048:2049, :], zrow.bitcast(BF16)[0:1, 0:1024],
                 waits=[t_zrb], eng="gpsimd")
    t_g0 = P.dma(grow_d.ap()[:, 0:2048], gmT[:], waits=[t_gtm], eng="gpsimd")
    _s1 = P.dsem["gpsimd"]
    _w1 = [(t_zr[0], t_zr[1]), (_s1, _s1.n)]
    _s1.n += 16
    def _zg(e, wl=list(_w1), s=_s1):
        for ws, wv in wl:
            e.wait_ge(ws.h, wv)
        e.dma_start(out=grow_d.ap()[:, 2048:2049], in_=zrow[:, 0:1],
                    allow_slow_non_contiguous=True).then_inc(s.h, 16)
    P.ops["gpsimd"].append(_zg)
    t_g1_ = (_s1, _s1.n)

    pu4 = {}
    for tt in range(16):
        slot, tq = tt // 4, tt % 4
        tl = pool_x.dma(a_tm, x3r_d.ap()[128 * tt:128 * tt + 128, :],
                        waits=[t for t in (pu4.get("a"),) if t])
        tbs = []
        for dmh in range(2):
            tmb = P.mm(B3[:, 0:512],
                       gmT[:, 512 * slot + 128 * tq:512 * slot + 128 * tq + 128],
                       b2sM[:, 512 * dmh:512 * dmh + 512], start=True, stop=True,
                       waits=[t_gtm, t_b2l, tau_done] + [t for t in (pu4.get("b2"),) if t])
            tadd = P.tt(a_tm[:, 512 * dmh:512 * dmh + 512],
                        a_tm[:, 512 * dmh:512 * dmh + 512], B3[:, 0:512],
                        ALU.add, waits=[tmb, tl])
            pu4["b2"] = tadd
            tbs.append(tadd)
        tda = None
        for dmh in range(2):
            tda = P.dma(x4a_d[dmh].ap()[128 * tt:128 * tt + 128, :],
                        a_tm[:, 512 * dmh:512 * dmh + 512], waits=tbs, eng="gpsimd")
        pu4["a"] = tda
    P.barrier()

    scat_prev = None
    pe4 = {}
    for e in range(NE):
        gv = g_tm.rearrange("p (s t e) -> p e s t", s=4, t=4, e=8)[:, e]
        w0 = [t_gtm] + [t for t in (pe4.get("m16"),) if t]
        t1 = P.ts(m16f, gv, 0.0, ALU.is_gt, waits=w0)
        t2 = P.copy(m16b, m16f, waits=[t1])
        t3 = P.mm(B0[:, 0:16], trlsb, m16b, start=True, stop=True,
                  waits=[t2, t_trl] + [t for t in (pe4.get("B0"),) if t])
        t4 = P.act(pfs, B0[:, 0:16], AF.Copy, waits=[t3] + [t for t in (pe4.get("pfs"),) if t])
        pe4["B0"] = t4
        t5 = P.mm(B1[0:16, 0:1], m16b, obs[:], start=True, stop=True,
                  waits=[t2] + [t for t in (pe4.get("B1"),) if t])
        t6 = P.act(tots, B1[0:16, 0:1], AF.Copy, waits=[t5] + [t for t in (pe4.get("tots"),) if t])
        pe4["B1"] = t6
        t7 = P.mm(B2[0:16, 0:1], ts16s, tots, start=True, stop=True,
                  waits=[t6, t_t16] + [t for t in (pe4.get("B2"),) if t])
        t8 = P.act(offs[:, 0:1], B2[0:16, 0:1], AF.Copy, waits=[t7] + [t for t in (pe4.get("offs"),) if t])
        pe4["B2"] = t8
        t9 = P.transpose(B2[0:1, 0:16], offs[:, 0:1], idIs[0:16, 0:16], waits=[t8])
        t10 = P.act(offrow, B2[0:1, 0:16], AF.Copy, waits=[t9] + [t for t in (pe4.get("orow"),) if t])
        pe4["offs"] = t10
        t11 = P.mm(B2[:, 0:16], o1s[:], offrow, start=True, stop=True, waits=[t10])
        t12 = P.ts(dsb, pfs, -1.0, ALU.add, waits=[t4] + [t for t in (pe4.get("dsb"),) if t])
        pe4["pfs"] = t12
        t13 = P.tt(dsb, dsb, B2[:, 0:16], ALU.add, waits=[t12, t11])
        pe4["B2"] = t13
        pe4["orow"] = t13
        t14 = P.ts(dsb, dsb, -2048.0, ALU.add, waits=[t13])
        t15 = P.tt(dsb, dsb, m16f, ALU.mult, waits=[t14])
        pe4["m16"] = t15
        t16 = P.ts(dsb, dsb, 2048.0, ALU.add, waits=[t15])
        t17 = P.copy(ddi, dsb, waits=[t16] + list(pe4.get("ddi") or ()))
        pe4["dsb"] = t17
        wpre = [t_c2k] + [t for t in (pe4.get("ixu"),) if t]
        tpre = P.dma(ix_d.ap()[e, 0:768, 0], c2ksb[0:1, :], waits=wpre, eng="gpsimd")
        sct = []
        for tile in range(16):
            wl = [tpre, t17, t_tio] + sct[-1:]
            sct.append(P.idma(pool_ix, (lambda e_, e=e, tile=tile: e_.indirect_dma_start(
                ix_d.ap().rearrange("e t o -> (e t) o"),
                IndirectOffsetOnAxis(ddi[:, tile:tile + 1], 0),
                tiosb[:, tile:tile + 1], None, element_offset=2049 * e)), waits=wl))
        pe4["ddi"] = sct[-4:]
        t_ix = []
        for g in range(6):
            t_ix.append(P.dma(ixsb[:, g:g + 1], ix_d.ap()[e, 128 * g:128 * g + 128, 0:1],
                              waits=sct + [t for t in (pe4.get("ixc"), scat_prev) if t]))
        pe4["ixu"] = t_ix[-1]
        tgg = []
        txg = []
        for g in range(6):
            wl = [t_ix[g], t_g0, t_g1_, t_z1] + [t for t in (pe4.get("gg"), pe4.get("xgf", )) if t]
            tgg.append(P.idma(pool_gx, (lambda e_, e=e, g=g: e_.indirect_dma_start(
                ggsb[:, g:g + 1], None,
                grow_d.ap().rearrange("e t -> (e t)").unsqueeze(1),
                IndirectOffsetOnAxis(ixsb[:, g:g + 1], 0),
                element_offset=2049 * e)), waits=wl))
            wl2 = [t_ix[g]] + [t for t in (pe4.get(("xgu", 0)),) if t]
            t_xgm = P.idma(pool_gx, (lambda e_, g=g: e_.indirect_dma_start(
                xg_tm, None, x3rb_d.ap(), IndirectOffsetOnAxis(ixsb[:, g:g + 1], 0))),
                waits=wl2)
            for ci in range(NCIN):
                wtr = [t_xgm, t_idb] + [t for t in (pe4.get("PT"),) if t]
                ttr = P.transpose(PT[:, 0:128], xg_tm[:, 128 * ci:128 * ci + 128],
                                  idIbs, waits=wtr)
                tevt = P.copy(xg_f[:, NCAP * ci + 128 * g:NCAP * ci + 128 * g + 128],
                              PT[:, 0:128],
                              waits=[ttr] + [t for t in (pe4.get("xgf2"),) if t])
                pe4["PT"] = tevt
                pe4["xgf2"] = tevt
                txg.append(tevt)
            pe4[("xgu", 0)] = pe4["PT"]
        pe4["gg"] = tgg[-1]
        pe4["ixc"] = pe4["PT"]
        pe4["xgf"] = pe4["PT"]
        ev_last = None
        slabsG = [(0, 512), (512, 256)]
        for hc in range(NHC):
            k = hc % 4
            wst_ = [pe4[("st", k)]] if ("st", k) in pe4 else []
            tl = pool_w.dma(r3(stgM[k], NCIN, 128),
                            w1_d[e][:, 128 * hc:128 * hc + 128]
                            .rearrange("(c p) h -> p c h", p=128), waits=wst_)
            tc = P.copy(w1bM[k][:], stgM[k],
                        waits=[tl] + ([pe4[("mmk", k)]] if ("mmk", k) in pe4 else []))
            pe4[("st", k)] = tc
            for hf, (s0, sw) in enumerate(slabsG):
                bank = pp[hf]
                w0 = [tc] + ([pe4[("hp", hf)]] if ("hp", hf) in pe4 else [])
                pairs = [(w1bM[k][:, 128 * ci:128 * ci + 128],
                          xg_f[:, NCAP * ci + s0:NCAP * ci + s0 + sw])
                         for ci in range(NCIN)]
                tmm = P.mmchain(bank[:, 0:sw], pairs, waits=w0 + txg[-1:])
                pe4[("mmk", k)] = tmm
                wev = [tmm, t_b1l]
                if hc == 0 and hf == 0 and ("ht",) in pe4:
                    wev.append(pe4[("ht",)])
                ev_last = P.act(htG[:, NCAP * hc + s0:NCAP * hc + s0 + sw],
                                bank[:, 0:sw], AF.Gelu,
                                bias=b1sM[:, NHC * e + hc:NHC * e + hc + 1],
                                waits=wev)
                pe4[("hp", hf)] = ev_last
        pyb = [B0, B1, B2, B3, B4, B5]
        for dmh in range(2):
            ymm = {}
            for hc in range(NHC):
                k = hc % 4
                wst_ = [pe4[("st", k)]] if ("st", k) in pe4 else []
                tl = pool_w.dma(stgM[k][:, 0:512],
                                w2_d[e][128 * hc:128 * hc + 128,
                                        512 * dmh:512 * dmh + 512], waits=wst_)
                tc = P.copy(w2bM[k][:], stgM[k][:, 0:512],
                            waits=[tl] + ([pe4[("ymk", k)]] if ("ymk", k) in pe4 else []))
                pe4[("st", k)] = tc
                for g in range(6):
                    w0 = [tc, ev_last]
                    if hc == 0 and ("py", g) in pe4:
                        w0.append(pe4[("py", g)])
                    ymm[g] = P.mm(pyb[g][:], htG[:, NCAP * hc + 128 * g:NCAP * hc + 128 * g + 128],
                                  w2bM[k][:], start=(hc == 0), stop=(hc == NHC - 1),
                                  waits=w0)
                    pe4[("ymk", k)] = ymm[g]
            for g in range(6):
                yslot = g % 2
                wyg = [ymm[g], tgg[g]] + [t for t in (pe4.get(("ygu", yslot)),) if t]
                tyg = P.ts(ygs2[yslot], pyb[g][:], ggsb[:, g:g + 1], ALU.mult, waits=wyg)
                pe4[("py", g)] = tyg
                wl3 = [tyg] + ([scat_prev] if scat_prev else [])
                scat_prev = P.idma(pool_sc, (lambda e_, g=g, dmh=dmh, yslot=yslot: e_.indirect_dma_start(
                    x4a_d[dmh].ap(),
                    IndirectOffsetOnAxis(ixsb[:, g:g + 1], 0),
                    ygs2[yslot], None, compute_op=ALU.add)), waits=wl3)
                pe4[("ygu", yslot)] = scat_prev
        pe4[("ht",)] = P.esem["tensor"].tag()
    P.barrier()
    P.wait("gpsimd", [scat_prev])

    pl = {}
    for tt in range(16):
        slot, tq = tt // 4, tt % 4
        tla = []
        for dmh in range(2):
            tla.append(pool_x.dma(a_tm[:, 512 * dmh:512 * dmh + 512],
                                  x4a_d[dmh].ap()[128 * tt:128 * tt + 128, :],
                                  waits=[scat_prev] + [t for t in (pl.get("a"),) if t]))
        av = a_tm
        w0 = list(tla) + ([pl["sq"]] if "sq" in pl else [])
        tsq = P.act(sqtM, av, AF.Square, waits=w0)
        tm_ = P.emit("vector", lambda e_, av=av: e_.tensor_reduce(
            out=mrow4[:, 0:1], in_=av, axis=mybir.AxisListType.X, op=ALU.add),
            waits=list(tla) + ([pl["m"]] if "m" in pl else []))
        ts2_ = P.emit("vector", lambda e_: e_.tensor_reduce(
            out=mrow4[:, 1:2], in_=sqtM, axis=mybir.AxisListType.X, op=ALU.add),
            waits=[tsq] + ([pl["s2"]] if "s2" in pl else []))
        pl["sq"] = ts2_
        t1 = P.ts(mrow4[:, 0:1], mrow4[:, 0:1], 1.0 / D, ALU.mult, waits=[tm_])
        t2 = P.ts(mrow4[:, 1:2], mrow4[:, 1:2], 1.0 / D, ALU.mult, waits=[ts2_])
        t3 = P.tt(mrow4[:, 2:3], mrow4[:, 0:1], mrow4[:, 0:1], ALU.mult,
                  waits=[t1] + ([pl["v"]] if "v" in pl else []))
        t4 = P.tt(mrow4[:, 2:3], mrow4[:, 1:2], mrow4[:, 2:3], ALU.subtract,
                  waits=[t2, t3])
        t5 = P.ts(mrow4[:, 2:3], mrow4[:, 2:3], 1e-5, ALU.add, waits=[t4])
        t6 = P.act(mrow4[:, 2:3], mrow4[:, 2:3], AF.Sqrt, waits=[t5])
        t7 = P.recip(mrow4[:, 3:4], mrow4[:, 2:3],
                     waits=[t6] + ([pl["r"]] if "r" in pl else []))
        t8 = P.ts(x4tM, av, mrow4[:, 0:1], ALU.subtract, s2=mrow4[:, 3:4],
                  op1=ALU.mult, waits=[t7, t1] + ([pl["x4"]] if "x4" in pl else []))
        pl["m"] = t8
        pl["s2"] = t8
        pl["v"] = t8
        pl["r"] = t8
        pl["a"] = t8
        td = P.dma(out_d[slot, 128 * tq:128 * tq + 128, :], x4tM,
                   waits=[t8], eng="gpsimd")
        pl["x4"] = td
    P.barrier()

    P.wait("gpsimd", P.all_tags())
    P.build()
    st.close()
    return nc


def kernel(**inputs):
    nc = build_fused(dbg=False)
    im = host_prep_fused(inputs)
    res = run_bass_kernel_spmd(nc, im, core_ids=list(range(8)))
    out = np.zeros((2, 8192, 1024), np.float32)
    for c in range(8):
        for i in range(4):
            seq, m = slab_of(c, i)
            out[seq, 512 * m:512 * m + 512] = np.array(res.results[c]["out"])[i]
    return out
